# version 17
# baseline (speedup 1.0000x reference)
import numpy as np
import ml_dtypes
from contextlib import ExitStack

import concourse.bass as bass
import concourse.mybir as mybir
from concourse.bass_utils import run_bass_kernel_spmd

F32 = mybir.dt.float32
BF16 = mybir.dt.bfloat16
ALU = mybir.AluOpType
AF = mybir.ActivationFunctionType
AX = mybir.AxisListType.X

FULL = dict(B=2, S=2048, D=1024, H=16, DFF=4096)


class Em:
    ENGINES = ("t", "s", "v", "q", "g")

    def __init__(self):
        self.ops = {e: [] for e in self.ENGINES}
        self.cnt = {}
        self.sems = {}

    def op(self, e, fn, inc=None, incn=1):
        v = None
        if inc is not None:
            v = self.cnt.get(inc, 0) + incn
            self.cnt[inc] = v

        def run(eng, fn=fn, inc=inc, incn=incn):
            r = fn(eng)
            if inc is not None:
                r.then_inc(self.sems[inc], incn)

        self.ops[e].append(run)
        return v

    def wait(self, e, sem, v):
        assert v <= self.cnt.get(sem, 0), f"wait {sem}>={v}, only {self.cnt.get(sem, 0)} planned"
        if v <= 0:
            return
        self.ops[e].append(lambda eng, sem=sem, v=v: eng.wait_ge(self.sems[sem], v))

    def drain(self, e):
        self.ops[e].append(lambda eng: eng.drain())

    def val(self, sem):
        return self.cnt.get(sem, 0)


def build(cfg):
    B, S, D, H, DFF = cfg["B"], cfg["S"], cfg["D"], cfg["H"], cfg["DFF"]
    DH, WIN, DIL = 64, 128, 4
    DC = D // 128
    HP = H // 2
    NT = S // 128
    ND = S // DIL
    NJD = ND // 128
    NW = S // WIN
    SH = S // 2
    NTH = NT // 2
    NWH = NW // 2
    NIS_D = (ND // 2) // 128
    TB = 512
    NB = S // TB
    TBT = TB // 128
    FT = DFF // 128
    DBW = min(512, D)
    NDB = D // DBW
    SB512 = S // 512
    assert H * DH == D and NWH % 4 == 0 and SH % 512 == 0 and ND % 256 == 0

    nc = bass.Bass("TRN2", target_bir_lowering=False)
    x_h = nc.dram_tensor("x", [B, S, D], F32, kind="ExternalInput").ap()
    wqkv_h = nc.dram_tensor("w_qkv", [D, 3 * D], F32, kind="ExternalInput").ap()
    wout_h = nc.dram_tensor("w_out", [D, D], F32, kind="ExternalInput").ap()
    wup_h = nc.dram_tensor("w_up", [DFF, D], F32, kind="ExternalInput").ap()
    wdn_h = nc.dram_tensor("w_down", [D, DFF], F32, kind="ExternalInput").ap()
    ident_h = nc.dram_tensor("ident", [128, 128], BF16, kind="ExternalInput").ap()
    out_h = nc.dram_tensor("out", [B, S, D], F32, kind="ExternalOutput").ap()
    xn_h = nc.dram_tensor("xn_scr", [B, S, D], BF16, kind="Internal").ap()
    s_h = nc.dram_tensor("s_scr", [B, S, D], BF16, kind="Internal").ap()
    wupt_h = nc.dram_tensor("wupt_scr", [DFF, D], BF16, kind="Internal").ap()
    wdnt_h = nc.dram_tensor("wdnt_scr", [D, DFF], BF16, kind="Internal").ap()

    em = Em()
    st = ExitStack()
    with st:
        sb = lambda name, shape, dt: st.enter_context(nc.sbuf_tensor(name, shape, dt))
        for sn in ("t", "s", "v", "qa", "qw0", "qw1", "qx0", "qx1", "qt0", "qt1",
                   "qu0", "qu1", "qd0", "qd1", "qd2", "qd3", "qd4", "qd5", "qs0", "qs1",
                   "gwa", "gwb", "gna", "gnb", "gsb", "gxa", "gxb", "goa", "gob"):
            em.sems[sn] = st.enter_context(nc.semaphore(sn))

        ident = sb("ident_sb", [128, 128], BF16)
        ones_cb = sb("ones_cb", [128, 1], BF16)
        one11 = sb("one11", [1, 1], F32)
        ones_cf = sb("ones_cf", [128, 1], F32)
        ones_rf = sb("ones_rf", [1, 128], F32)
        thrp = sb("thrp", [128, 1], F32)
        thrn = sb("thrn", [128, 1], F32)
        thr11 = sb("thr11", [1, 1], F32)
        stat_nm = sb("stat_nm", [128, NT], F32)
        stat_is = sb("stat_is", [128, NT], F32)
        stat_a1 = sb("stat_a1", [128, NT], F32)
        scr_a = sb("scr_a", [128, NT], F32)
        scr_b = sb("scr_b", [128, NT], F32)
        red1 = sb("red1", [128, 1], F32)
        trash = sb("trash", [128, D], BF16)

        em.op("q", lambda e: e.dma_start(out=ident[:], in_=ident_h[:]), inc="qa", incn=16)
        ld_ident = em.val("qa")
        em.op("v", lambda e: e.memset(ones_cb[:], 1.0), inc="v")
        em.op("v", lambda e: e.memset(one11[0:1, :], 1.0), inc="v")
        em.op("v", lambda e: e.memset(ones_cf[:], 1.0), inc="v")
        em.op("v", lambda e: e.memset(ones_rf[0:1, :], 1.0), inc="v")

        CMAX = max(D, DFF)
        with ExitStack() as p2:
            sb2 = lambda name, shape, dt: p2.enter_context(nc.sbuf_tensor(name, shape, dt))
            wch = [sb2(f"wch{i}", [128, CMAX], F32) for i in range(3)]
            btrash = sb2("btrash", [128, CMAX], BF16)
            wtc = [sb2(f"wtc{i}", [128, CMAX], BF16) for i in range(2)]
            ncmp = sb2("ncmp", [128, CMAX], F32)
            asums = sb2("asums", [128, 32], F32)
            p_red = p2.enter_context(nc.psum_tensor("p_red", [128, 2], F32))

            wch_guard = [("s", 0)] * 3
            wtc_guard = [0, 0]
            p1q = [("q", "qw0"), ("g", "qw1"), ("s", "qd0")]
            p2q = [("q", "qd2"), ("s", "qd3")]
            qsem = ["qw0", "qw1"]
            for wi, (w_src, wt_dst, R, C) in enumerate(
                    [(wup_h, wupt_h, DFF, D), (wdn_h, wdnt_h, D, DFF)]):
                NCH = R // 128
                for c in range(NCH):
                    par = c % 3
                    leng, lsem = p1q[par]
                    geng, gval = wch_guard[par]
                    em.wait(leng, geng, gval)
                    em.op(leng, lambda e, c=c, par=par, w_src=w_src, C=C: e.dma_start(
                        out=wch[par][:, 0:C], in_=w_src[c * 128:(c + 1) * 128, :]),
                        inc=lsem, incn=16)
                    em.wait("s", lsem, em.val(lsem))
                    em.drain("s")
                    wch_guard[par] = ("s", em.op(
                        "s", lambda e, c=c, par=par, C=C: e.activation(
                            btrash[:, 0:C], wch[par][:, 0:C], AF.Abs,
                            accum_out=asums[:, c:c + 1]), inc="s"))
                em.wait("v", "s", em.val("s"))
                em.drain("v")
                em.op("v", lambda e, NCH=NCH: e.reduce_sum(red1[:, 0:1], asums[:, 0:NCH], AX),
                      inc="v")
                em.wait("t", "v", em.val("v"))
                em.wait("t", "qa", ld_ident)
                em.op("t", lambda e: e.matmul(p_red[0:1, 0:1], ones_cf[:, 0:1], red1[:, 0:1],
                                              start=True, stop=True), inc="t")
                em.wait("v", "t", em.val("t"))
                em.op("v", lambda e, R=R, C=C: e.tensor_scalar(
                    thr11[0:1, 0:1], p_red[0:1, 0:1], 0.5 / (R * C), None, ALU.mult), inc="v")
                em.wait("t", "v", em.val("v"))
                em.op("t", lambda e: e.matmul(p_red[:, 1:2], ones_rf[0:1, :], thr11[0:1, 0:1],
                                              start=True, stop=True), inc="t")
                em.wait("v", "t", em.val("t"))
                em.op("v", lambda e: e.tensor_copy(thrp[:, 0:1], p_red[:, 1:2]), inc="v")
                em.drain("v")
                em.op("v", lambda e: e.tensor_scalar(thrn[:, 0:1], thrp[:, 0:1], -1.0, None,
                                                     ALU.mult), inc="v")
                em.drain("v")
                for c in range(NCH):
                    par = c % 3
                    spar = c % 2
                    leng, lsem = p2q[c % 2]
                    geng, gval = wch_guard[par]
                    em.wait(leng, geng, gval)
                    em.op(leng, lambda e, c=c, par=par, w_src=w_src, C=C: e.dma_start(
                        out=wch[par][:, 0:C], in_=w_src[c * 128:(c + 1) * 128, :]),
                        inc=lsem, incn=16)
                    em.wait("v", lsem, em.val(lsem))
                    em.drain("v")
                    em.op("v", lambda e, par=par, C=C: e.tensor_single_scalar(
                        ncmp[:, 0:C], wch[par][:, 0:C], thrn[:, 0:1], ALU.is_lt), inc="v")
                    em.drain("v")
                    em.wait("v", ["gwa", "gwb"][spar], wtc_guard[spar])
                    wch_guard[par] = ("v", em.op(
                        "v", lambda e, par=par, spar=spar, C=C: e.scalar_tensor_tensor(
                            wtc[spar][:, 0:C], wch[par][:, 0:C], thrp[:, 0:1], ncmp[:, 0:C],
                            ALU.is_gt, ALU.subtract), inc="v"))
                    em.wait("g", "v", wch_guard[par][1])
                    wtc_guard[spar] = em.op(
                        "g", lambda e, c=c, spar=spar, wt_dst=wt_dst, C=C: e.dma_start(
                            out=wt_dst[c * 128:(c + 1) * 128, :], in_=wtc[spar][:, 0:C]),
                        inc=["gwa", "gwb"][spar], incn=16)
            tern_gw = (em.val("gwa"), em.val("gwb"))

        for s in range(B):
            def _do_seq(s=s):
                qx, qt, qu = f"qx{s}", f"qt{s}", f"qu{s}"
                with ExitStack() as sq:
                    sbq = lambda name, shape, dt: sq.enter_context(nc.sbuf_tensor(name, shape, dt))
                    with ExitStack() as p3:
                        xs = p3.enter_context(nc.sbuf_tensor(f"xs{s}", [128, NT, D], F32))
                        xnb = [p3.enter_context(nc.sbuf_tensor(f"xnb{s}{i}", [128, D], BF16))
                               for i in range(2)]
                        for gsem_ in ("v", "s", "t", "gwa", "gwb", "gna", "gnb", "gsb", "goa", "gob"):
                            em.wait("q", gsem_, em.val(gsem_))
                        for i in range(4):
                            em.op("q", lambda e, s=s, i=i: e.dma_start(
                                out=xs[:, i * (NT // 4):(i + 1) * (NT // 4), :],
                                in_=x_h[s].rearrange("(t p) d -> p t d", p=128)[
                                    :, i * (NT // 4):(i + 1) * (NT // 4), :]), inc=qx, incn=16)
                        x_ld = em.val(qx)
                        em.wait("s", qx, x_ld)
                        em.wait("v", qx, x_ld)
                        for tt in range(NT):
                            em.drain("s")
                            em.op("s", lambda e, tt=tt: e.activation(
                                trash[:, 0:D], xs[:, tt, :], AF.Square,
                                accum_out=scr_a[:, tt:tt + 1]), inc="s")
                            em.op("v", lambda e, tt=tt: e.reduce_sum(
                                scr_b[:, tt:tt + 1], xs[:, tt, :], AX), inc="v")
                            em.drain("v")
                        em.drain("v")
                        em.op("v", lambda e: e.tensor_scalar(stat_nm[:], scr_b[:], -1.0 / D, None,
                                                             ALU.mult), inc="v")
                        em.wait("v", "s", em.val("s"))
                        em.drain("v")
                        em.op("v", lambda e: e.tensor_scalar(scr_a[:], scr_a[:], 1.0 / D, None,
                                                             ALU.mult), inc="v")
                        em.drain("v")
                        em.op("v", lambda e: e.tensor_tensor(scr_b[:], stat_nm[:], stat_nm[:],
                                                             ALU.mult), inc="v")
                        em.drain("v")
                        em.op("v", lambda e: e.tensor_tensor(scr_a[:], scr_a[:], scr_b[:],
                                                             ALU.subtract), inc="v")
                        em.drain("v")
                        em.op("v", lambda e: e.tensor_scalar(scr_b[:], scr_a[:], 1e-5, None,
                                                             ALU.add), inc="v")
                        em.drain("v")
                        em.op("v", lambda e: e.reciprocal(scr_b[:], scr_b[:]), inc="v")
                        em.drain("v")
                        em.op("v", lambda e: e.tensor_tensor(scr_a[:], scr_a[:], scr_b[:],
                                                             ALU.mult), inc="v")
                        em.drain("v")
                        vst = em.val("v")
                        em.wait("s", "v", vst)
                        em.drain("s")
                        em.op("s", lambda e: e.activation(stat_is[:], scr_b[:], AF.Sqrt), inc="s")
                        em.drain("s")
                        em.op("s", lambda e: e.activation(scr_a[:], scr_a[:], AF.Sqrt), inc="s")
                        em.wait("v", "s", em.val("s"))
                        em.drain("v")
                        em.op("v", lambda e: e.tensor_scalar(scr_a[:], scr_a[:], 1e-8, None,
                                                             ALU.add), inc="v")
                        em.drain("v")
                        em.op("v", lambda e: e.reciprocal(stat_a1[:], scr_a[:]), inc="v")
                        em.drain("v")
                        em.wait("v", "gwa", tern_gw[0])
                        em.wait("v", "gwb", tern_gw[1])
                        em.wait("v", "go", em.val("go"))
                        em.wait("v", "t", em.val("t"))
                        for tt in range(NT):
                            par = tt % 2
                            gsem = ["gna", "gnb"][par]
                            if tt >= 2:
                                em.wait("v", gsem, em.val(gsem))
                            em.drain("v")
                            em.op("v", lambda e, tt=tt, par=par: e.tensor_scalar(
                                xnb[par][:], xs[:, tt, :], stat_nm[:, tt:tt + 1],
                                stat_is[:, tt:tt + 1], ALU.add, ALU.mult), inc="v")
                            em.wait("g", "v", em.val("v"))
                            em.op("g", lambda e, s=s, tt=tt, par=par, gsem=gsem: e.dma_start(
                                out=xn_h[s].rearrange("(t p) d -> p t d", p=128)[:, tt, :],
                                in_=xnb[par][:]), inc=gsem, incn=16)
                        xn_done = (em.val("gna"), em.val("gnb"))

                    pa = ExitStack()
                    sba = lambda name, shape, dt: pa.enter_context(nc.sbuf_tensor(name, shape, dt))
                    qT = sba(f"qT{s}", [128, HP, S], BF16)
                    kT = sba(f"kT{s}", [128, HP, S], BF16)
                    v_c = sba(f"v_c{s}", [128, NT, H * 65], BF16)
                    v_d = sba(f"v_d{s}", [128, NT, H * 65], BF16)
                    for gsem_ in ("t", "s", "gna", "gnb", "gwa", "gwb", "gsb", "go"):
                        em.wait("v", gsem_, em.val(gsem_))
                    em.op("v", lambda e: e.memset(v_c[:], 3.0), inc="v")
                    em.op("v", lambda e: e.memset(v_d[:], 3.0), inc="v")
                    vone_v = em.val("v")
                    with ExitStack() as p4:
                        xnT = p4.enter_context(nc.sbuf_tensor(f"xnT{s}", [128, DC, S], BF16))
                        wsl = [p4.enter_context(nc.sbuf_tensor(f"wsl{s}{i}", [128, D], F32))
                               for i in range(2)]
                        wqs = [p4.enter_context(nc.sbuf_tensor(f"wqs{s}{i}", [128, DC, D], BF16))
                               for i in range(2)]
                        p_qkv = [p4.enter_context(nc.psum_tensor(f"pqkv{s}{i}", [128, 512], F32))
                                 for i in range(2)]
                        em.wait("q", "gna", xn_done[0])
                        em.wait("q", "gnb", xn_done[1])
                        for dc in range(DC):
                            em.op("q", lambda e, s=s, dc=dc: e.dma_start_transpose(
                                out=xnT[:, dc, :], in_=xn_h[s][:, dc * 128:(dc + 1) * 128]),
                                inc=qt, incn=16)
                        xnT_ld = em.val(qt)

                        cast_v = [0, 0]
                        evac_eng = [0]

                        def qkv_evac(out_f, in_f):
                            eng = "v" if evac_eng[0] % 2 == 0 else "s"
                            evac_eng[0] += 1
                            em.wait(eng, "t", em.val("t"))
                            em.drain(eng)
                            if eng == "v":
                                cnt = em.op("v", lambda e, o=out_f, i=in_f: e.tensor_copy(o(), i()),
                                            inc="v")
                            else:
                                cnt = em.op("s", lambda e, o=out_f, i=in_f: e.copy(o(), i()),
                                            inc="s")
                            return (eng, cnt)

                        def load_wslice(foff, slot, t_guard):
                            for dc in range(DC):
                                par = dc % 2
                                em.wait("q", "t", t_guard)
                                em.wait("q", "v", cast_v[par])
                                em.op("q", lambda e, dc=dc, par=par, foff=foff: e.dma_start(
                                    out=wsl[par][:], in_=wqkv_h[dc * 128:(dc + 1) * 128,
                                                               foff:foff + D]),
                                    inc=qsem[par], incn=16)
                                em.wait("v", qsem[par], em.val(qsem[par]))
                                cast_v[par] = em.op(
                                    "v", lambda e, dc=dc, par=par, slot=slot: e.tensor_copy(
                                        wqs[slot][:, dc, :], wsl[par][:]), inc="v")
                            return em.val("v")

                        qsem = ["qw0", "qw1"]
                        pq_free = [("v", 0), ("v", 0)]
                        em.wait("t", qt, xnT_ld)
                        mmc = 0
                        cast_done = [0, 0]
                        cast_done[0] = load_wslice(0, 0, em.val("t"))
                        cast_done[1] = load_wslice(D, 1, em.val("t"))
                        for qk in range(2):
                            em.wait("t", "v", cast_done[qk])
                            dst = qT if qk == 0 else kT
                            for c in range(HP):
                                for tb in range(SB512):
                                    par = mmc % 2
                                    mmc += 1
                                    em.wait("t", *pq_free[par])
                                    for dc in range(DC):
                                        em.op("t", lambda e, c=c, tb=tb, dc=dc, par=par, qk=qk:
                                              e.matmul(
                                                  p_qkv[par][:, :],
                                                  wqs[qk][:, dc, c * 128:(c + 1) * 128],
                                                  xnT[:, dc, tb * 512:(tb + 1) * 512],
                                                  start=(dc == 0), stop=(dc == DC - 1)),
                                              inc="t" if dc == DC - 1 else None)
                                    pq_free[par] = qkv_evac(
                                        lambda c=c, tb=tb, dst=dst: dst[:, c, tb * 512:(tb + 1) * 512],
                                        lambda par=par: p_qkv[par][:, :])
                            if qk == 0:
                                q_mms_done = em.val("t")
                        vcast = load_wslice(2 * D, 0, q_mms_done)
                        em.wait("t", "v", vcast)
                        em.wait("t", "v", vone_v)
                        for which, dstv in ((0, v_c), (1, v_d)):
                            for tt in range(NT):
                                for fb in range(NDB):
                                    par = mmc % 2
                                    mmc += 1
                                    em.wait("t", *pq_free[par])
                                    for dc in range(DC):
                                        if which == 0:
                                            lhs_ap = (lambda dc=dc, tt=tt:
                                                      xnT[:, dc, tt * 128:(tt + 1) * 128])
                                        else:
                                            g0_, j0_ = (tt * 128) // ND, (tt * 128) % ND
                                            lhs_ap = (lambda dc=dc, g0_=g0_, j0_=j0_: xnT[
                                                :, dc, g0_ + DIL * j0_: g0_ + DIL * (j0_ + 127) + 1: DIL])
                                        em.op("t", lambda e, lhs_ap=lhs_ap, fb=fb, dc=dc, par=par:
                                              e.matmul(p_qkv[par][:, 0:DBW], lhs_ap(),
                                                       wqs[0][:, dc, fb * DBW:(fb + 1) * DBW],
                                                       start=(dc == 0), stop=(dc == DC - 1)),
                                              inc="t" if dc == DC - 1 else None)
                                    nh = DBW // 64
                                    pq_free[par] = qkv_evac(
                                        lambda tt=tt, fb=fb, dstv=dstv, nh=nh:
                                        dstv[:, tt].rearrange("p (h c) -> p h c", c=65)[
                                            :, fb * nh:(fb + 1) * nh, 0:64],
                                        lambda par=par: p_qkv[par][:, 0:DBW].rearrange(
                                            "p (h c) -> p h c", c=64))
                    qkv_t = em.val("t")
                    qkv_done = (em.val("v"), em.val("s"))

                    s_half = sba(f"s_half{s}", [128, NTH, D], BF16)
                    woutb = sba(f"woutb{s}", [128, DC, D], BF16)
                    wslk = [sba(f"wslk{s}{i}", [128, D], F32) for i in range(2)]
                    wob_v = [0, 0]
                    em.wait("q", "t", qkv_t)
                    em.wait("q", "v", em.val("v"))
                    for dc in range(DC):
                        par = dc % 2
                        em.wait("q", "v", wob_v[par])
                        em.op("q", lambda e, dc=dc, par=par: e.dma_start(
                            out=wslk[par][:], in_=wout_h[dc * 128:(dc + 1) * 128, :]),
                            inc=["qw0", "qw1"][par], incn=16)
                        em.wait("v", ["qw0", "qw1"][par], em.val(["qw0", "qw1"][par]))
                        wob_v[par] = em.op("v", lambda e, dc=dc, par=par: e.tensor_copy(
                            woutb[:, dc, :], wslk[par][:]), inc="v")
                    wout_cast = em.val("v")

                    for half in range(2):
                        def _do_half(half=half):
                            with ExitStack() as ph:
                                onormT = ph.enter_context(
                                    nc.sbuf_tensor(f"onT{s}{half}", [128, DC, SH], BF16))
                                with ExitStack() as p5:
                                    etb = [p5.enter_context(
                                        nc.sbuf_tensor(f"et{s}{half}{i}", [128, 512], BF16))
                                        for i in range(4)]
                                    otr = [p5.enter_context(
                                        nc.sbuf_tensor(f"otr{s}{half}{i}", [65, 512], BF16))
                                        for i in range(4)]
                                    onrm = [p5.enter_context(
                                        nc.sbuf_tensor(f"onrm{s}{half}{i}", [128, 256], BF16))
                                        for i in range(2)]
                                    rec = [p5.enter_context(
                                        nc.sbuf_tensor(f"rec{s}{half}{i}", [128, 4], F32))
                                        for i in range(2)]
                                    p_sc = [p5.enter_context(
                                        nc.psum_tensor(f"psc{s}{half}{h}{i}", [128, 512], F32))
                                        for h in range(2) for i in range(2)]
                                    p_oT = [p5.enter_context(
                                        nc.psum_tensor(f"poT{s}{half}{i}", [65, 512], F32))
                                        for i in range(2)]
                                    p_tr = p5.enter_context(
                                        nc.psum_tensor(f"ptr{s}{half}", [128, 260], F32))
                                    p_bk = p5.enter_context(
                                        nc.psum_tensor(f"pbk{s}{half}", [128, 512], F32))

                                    gd = dict(psc=[[0, 0], [0, 0]], evac=[0, 0],
                                              otr_g=[[0, 0], [0, 0]],
                                              onrm=[0, 0], ptr=0, pbk=0, av=[0, 0])
                                    pend = [None, None]
                                    ucnt = [0]

                                    def emit_evacs(NIS, upar):
                                        NIC = NIS * 128
                                        ev = [0, 0]
                                        for hi in range(2):
                                            em.wait("v", "t", gd["av"][hi])
                                            em.wait("v", "t", gd["otr_g"][hi][upar])
                                            em.drain("v")
                                            ev[hi] = em.op(
                                                "v", lambda e, hi=hi, NIC=NIC, upar=upar: e.tensor_copy(
                                                    otr[hi * 2 + upar][0:65, 0:NIC],
                                                    p_oT[hi][0:65, 0:NIC]), inc="v")
                                        gd["evac"] = ev[:]
                                        return ev

                                    def tail_fwd_one(NIS, upar, ev, hi, prev_ts):
                                        em.wait("t", "v", ev[hi])
                                        em.wait("t", "v", gd["ptr"] if hi == 0 else prev_ts)
                                        for k in range(NIS):
                                            em.op("t", lambda e, hi=hi, k=k, upar=upar: e.matmul(
                                                p_tr[:, k * 65:(k + 1) * 65],
                                                otr[hi * 2 + upar][0:65, k * 128:(k + 1) * 128],
                                                ident[0:65, 0:65], start=True, stop=True),
                                                inc="t" if k == NIS - 1 else None)
                                        gd["otr_g"][hi][upar] = em.val("t")
                                        em.wait("v", "t", em.val("t"))
                                        em.wait("v", "t", gd["onrm"][hi])
                                        em.drain("v")
                                        em.op("v", lambda e, hi=hi, NIS=NIS: e.reciprocal(
                                            rec[hi][:, 0:NIS], p_tr[:, 64:NIS * 65:65]), inc="v")
                                        em.drain("v")
                                        for k in range(NIS):
                                            em.op("v", lambda e, hi=hi, k=k: e.tensor_scalar(
                                                onrm[hi][:, k * 64:(k + 1) * 64],
                                                p_tr[:, k * 65:k * 65 + 64],
                                                rec[hi][:, k:k + 1], None, ALU.mult), inc="v")
                                            em.drain("v")
                                        return em.val("v")

                                    def tail_rest(NIS, merge, ts_done):
                                        em.wait("t", "v", ts_done[1])
                                        em.wait("t", "v", gd["pbk"])
                                        for hi in range(2):
                                            for k in range(NIS):
                                                em.op("t", lambda e, hi=hi, k=k: e.matmul(
                                                    p_bk[hi * 64:(hi + 1) * 64, k * 128:(k + 1) * 128],
                                                    onrm[hi][:, k * 64:(k + 1) * 64], ident[:, 0:128],
                                                    start=True, stop=True, tile_position=(0, hi * 64)),
                                                    inc="t" if (hi == 1 and k == NIS - 1) else None)
                                        bk_t = em.val("t")
                                        gd["onrm"] = [bk_t, bk_t]
                                        em.wait("v", "t", bk_t)
                                        em.drain("v")
                                        merge()
                                        gd["pbk"] = em.val("v")

                                    def run_pend_fwd():
                                        if pend[0] is not None:
                                            NIS, merge, upar, ev, st0 = pend[0]
                                            if st0 is None:
                                                ts0 = tail_fwd_one(NIS, upar, ev, 0, 0)
                                                pend[0] = (NIS, merge, upar, ev, ts0)
                                            else:
                                                ts1 = tail_fwd_one(NIS, upar, ev, 1, st0)
                                                gd["ptr"] = ts1
                                                pend[0] = None
                                                pend[1] = (NIS, merge, (st0, ts1))

                                    def run_pend_rest():
                                        run_pend_fwd()
                                        run_pend_fwd()
                                        if pend[1] is not None:
                                            NIS, merge, ts_done = pend[1]
                                            pend[1] = None
                                            tail_rest(NIS, merge, ts_done)

                                    def finish_unit(NIS, merge):
                                        upar = ucnt[0] % 2
                                        ucnt[0] += 1
                                        run_pend_rest()
                                        ev = emit_evacs(NIS, upar)
                                        pend[0] = (NIS, merge, upar, ev, None)

                                    def score_av_loop(hp, jlist, NIS, sc_ap, av_st, locmode=False,
                                                      mid_hook=None):
                                        NIC = NIS * 128
                                        hA, hB = 2 * hp, 2 * hp + 1
                                        NJ = len(jlist)
                                        if locmode:
                                            tsc = [0, 0]
                                            for jn, j in enumerate(jlist):
                                                for hi, h in enumerate((hA, hB)):
                                                    if jn == 0:
                                                        em.wait("t", "s", gd["psc"][hi][0])
                                                    lhsT, rhs, c0 = sc_ap(h, j)
                                                    em.op("t", lambda e, lhsT=lhsT, rhs=rhs, c0=c0, hi=hi:
                                                          e.matmul(p_sc[hi * 2][:, c0:c0 + 128],
                                                                   lhsT(), rhs(),
                                                                   start=True, stop=True), inc="t")
                                                    tsc[hi] = em.val("t")
                                            exps = [0, 0]
                                            for hi in range(2):
                                                em.wait("s", "t", tsc[hi])
                                                em.op("s", lambda e, hi=hi, NIC=NIC: e.activation(
                                                    etb[hi * 2][:, 0:NIC], p_sc[hi * 2][:, 0:NIC],
                                                    AF.Exp, scale=0.125), inc="s")
                                                exps[hi] = em.val("s")
                                                gd["psc"][hi][0] = exps[hi]
                                            if mid_hook is not None:
                                                mid_hook()
                                                mid_hook()
                                            for hi, h in enumerate((hA, hB)):
                                                em.wait("t", "s", exps[hi])
                                                em.wait("t", "v", gd["evac"][hi])
                                                for jn, j in enumerate(jlist):
                                                    em.op("t", lambda e, h=h, hi=hi, j=j, jn=jn: e.matmul(
                                                        p_oT[hi][0:65, jn * 128:(jn + 1) * 128],
                                                        av_st(h, j)(),
                                                        etb[hi * 2][:, jn * 128:(jn + 1) * 128],
                                                        start=True, stop=True),
                                                        inc="t" if jn == NJ - 1 else None)
                                                gd["av"][hi] = em.val("t")
                                            return
                                        tsc = [[0] * NJ for _ in range(2)]
                                        exps = [[0] * NJ for _ in range(2)]

                                        def emit_av(jn):
                                            par = jn % 2
                                            for hi, h in enumerate((hA, hB)):
                                                em.wait("t", "s", exps[hi][jn])
                                                if jn == 0:
                                                    em.wait("t", "v", gd["evac"][hi])
                                                em.op("t", lambda e, h=h, hi=hi, j=jlist[jn], par=par,
                                                      jn=jn, NIC=NIC: e.matmul(
                                                          p_oT[hi][0:65, 0:NIC], av_st(h, j)(),
                                                          etb[hi * 2 + par][:, 0:NIC],
                                                          start=(jn == 0), stop=(jn == NJ - 1)),
                                                      inc="t" if jn == NJ - 1 else None)
                                                if jn == NJ - 1:
                                                    gd["av"][hi] = em.val("t")

                                        for jn, j in enumerate(jlist):
                                            par = jn % 2
                                            for hi, h in enumerate((hA, hB)):
                                                if jn >= 2:
                                                    em.wait("t", "s", exps[hi][jn - 2])
                                                else:
                                                    em.wait("t", "s", gd["psc"][hi][par])
                                                lhsT, rhs, _ = sc_ap(h, j)
                                                em.op("t", lambda e, lhsT=lhsT, rhs=rhs, hi=hi, par=par,
                                                      NIC=NIC: e.matmul(
                                                          p_sc[hi * 2 + par][:, 0:NIC], lhsT(), rhs(),
                                                          start=True, stop=True), inc="t")
                                                tsc[hi][jn] = em.val("t")
                                            for hi in range(2):
                                                em.wait("s", "t", tsc[hi][jn])
                                                em.op("s", lambda e, hi=hi, par=par, NIC=NIC: e.activation(
                                                    etb[hi * 2 + par][:, 0:NIC],
                                                    p_sc[hi * 2 + par][:, 0:NIC],
                                                    AF.Exp, scale=0.125), inc="s")
                                                exps[hi][jn] = em.val("s")
                                                gd["psc"][hi][par] = exps[hi][jn]
                                            if jn >= 1:
                                                emit_av(jn - 1)
                                            if mid_hook is not None and (
                                                    jn == min(1, NJ - 2) or jn == min(3, NJ - 1)):
                                                mid_hook()
                                        emit_av(NJ - 1)

                                    hd = lambda h: ((h % 2) * 64, h // 2)

                                    em.wait("t", "v", em.val("v"))
                                    em.wait("t", "s", em.val("s"))
                                    for hp in range(HP):
                                        for u in range(NWH // 4):
                                            w0 = half * NWH + u * 4
                                            wl = list(range(w0, w0 + 4))

                                            def sc_ap(h, w, w0=w0):
                                                p0, c = hd(h)
                                                return (lambda p0=p0, c=c, w=w:
                                                        kT[p0:p0 + 64, c, w * 128:(w + 1) * 128],
                                                        lambda p0=p0, c=c, w=w:
                                                        qT[p0:p0 + 64, c, w * 128:(w + 1) * 128],
                                                        (w - w0) * 128)

                                            def av_st(h, w):
                                                return lambda h=h, w=w: v_c[:, w, h * 65:h * 65 + 65]

                                            score_av_loop(hp, wl, 4, sc_ap, av_st, locmode=True,
                                                          mid_hook=run_pend_fwd)

                                            def merge(hp=hp, w0=w0, half=half):
                                                c0 = (w0 - half * NWH) * 128
                                                em.op("v", lambda e, hp=hp, c0=c0: e.tensor_copy(
                                                    onormT[:, hp, c0:c0 + 512], p_bk[:, 0:512]), inc="v")
                                                em.drain("v")
                                            finish_unit(4, merge)
                                        for u in range(SH // 512):
                                            i0 = half * SH + u * 512
                                            jl = list(range(NT))

                                            def sc_ap(h, j, i0=i0):
                                                p0, c = hd(h)
                                                return (lambda p0=p0, c=c, j=j:
                                                        kT[p0:p0 + 64, c, j * 128:(j + 1) * 128],
                                                        lambda p0=p0, c=c, i0=i0:
                                                        qT[p0:p0 + 64, c, i0:i0 + 512], 0)

                                            def av_st(h, j):
                                                return lambda h=h, j=j: v_c[:, j, h * 65:h * 65 + 65]

                                            score_av_loop(hp, jl, 4, sc_ap, av_st, mid_hook=run_pend_fwd)

                                            def merge(hp=hp, i0=i0, half=half):
                                                c0 = i0 - half * SH
                                                em.op("v", lambda e, hp=hp, c0=c0: e.scalar_tensor_tensor(
                                                    onormT[:, hp, c0:c0 + 512], p_bk[:, 0:512], 1.0,
                                                    onormT[:, hp, c0:c0 + 512], ALU.mult, ALU.add),
                                                    inc="v")
                                                em.drain("v")
                                            finish_unit(4, merge)
                                        for g in range(DIL):
                                            ip0 = half * (ND // 2)
                                            jl = list(range(NJD))

                                            def sc_ap(h, j, g=g, ip0=ip0):
                                                p0, c = hd(h)
                                                return (lambda p0=p0, c=c, j=j, g=g: kT[
                                                    p0:p0 + 64, c,
                                                    g + DIL * (j * 128): g + DIL * (j * 128 + 127) + 1: DIL],
                                                    lambda p0=p0, c=c, g=g, ip0=ip0: qT[
                                                        p0:p0 + 64, c,
                                                        g + DIL * ip0: g + DIL * (ip0 + NIS_D * 128 - 1) + 1: DIL],
                                                    0)

                                            def av_st(h, j, g=g):
                                                return (lambda h=h, j=j, g=g:
                                                        v_d[:, g * NJD + j, h * 65:h * 65 + 65])

                                            score_av_loop(hp, jl, NIS_D, sc_ap, av_st, mid_hook=run_pend_fwd)

                                            def merge(hp=hp, g=g, ip0=ip0, half=half):
                                                for k in range(NIS_D):
                                                    c0 = g + DIL * (ip0 + k * 128) - half * SH
                                                    em.op("v", lambda e, hp=hp, c0=c0, k=k:
                                                          e.scalar_tensor_tensor(
                                                              onormT[:, hp, c0:c0 + DIL * 127 + 1:DIL],
                                                              p_bk[:, k * 128:(k + 1) * 128], 1.0,
                                                              onormT[:, hp, c0:c0 + DIL * 127 + 1:DIL],
                                                              ALU.mult, ALU.add), inc="v")
                                                    em.drain("v")
                                            finish_unit(NIS_D, merge)

                                    run_pend_fwd()
                                    run_pend_rest()
                                attn_v = em.val("v")
                                with ExitStack() as p6:
                                    p_s6 = [p6.enter_context(
                                        nc.psum_tensor(f"ps6{s}{half}{i}", [128, DBW], F32))
                                        for i in range(2)]
                                    em.wait("t", "v", attn_v)
                                    em.wait("t", "s", em.val("s"))
                                    em.wait("t", "v", wout_cast)
                                    if half == 1 or s == 1:
                                        em.wait("v", "gsb", em.val("gsb"))
                                    ps_free = [0, 0]
                                    pc = 0
                                    for tt in range(NTH):
                                        for db in range(NDB):
                                            par = pc % 2
                                            pc += 1
                                            em.wait("t", "v", ps_free[par])
                                            for dc in range(DC):
                                                em.op("t", lambda e, tt=tt, db=db, dc=dc, par=par:
                                                      e.matmul(p_s6[par][:, :],
                                                               onormT[:, dc, tt * 128:(tt + 1) * 128],
                                                               woutb[:, dc, db * DBW:(db + 1) * DBW],
                                                               start=(dc == 0), stop=(dc == DC - 1)),
                                                      inc="t" if dc == DC - 1 else None)
                                            em.wait("v", "t", em.val("t"))
                                            em.drain("v")
                                            ps_free[par] = em.op(
                                                "v", lambda e, tt=tt, db=db, par=par: e.tensor_copy(
                                                    s_half[:, tt, db * DBW:(db + 1) * DBW],
                                                    p_s6[par][:, :]), inc="v")
                                    em.wait("g", "v", em.val("v"))
                                    em.op("g", lambda e, s=s, half=half: e.dma_start(
                                        out=s_h[s].rearrange("(t p) d -> p t d", p=128)[
                                            :, half * NTH:(half + 1) * NTH, :],
                                        in_=s_half[:]), inc="gsb", incn=16)

                        _do_half()
                    pa.close()
                    with ExitStack() as p7:
                        sb7 = lambda name, shape, dt: p7.enter_context(
                            nc.sbuf_tensor(name, shape, dt))
                        wupT = sb7(f"wupT{s}", [128, DC, DFF], BF16)
                        xblk = [sb7(f"xblk{s}{i}", [128, TBT, D], F32) for i in range(2)]
                        xnblk = [sb7(f"xnblk{s}{i}", [128, TBT, D], BF16) for i in range(2)]
                        xnormT = sb7(f"xnormT{s}", [128, DC, TB], BF16)
                        h1gT = sb7(f"h1gT{s}", [128, FT, TB], BF16)
                        hsq = [sb7(f"hsq{s}{i}", [128, TB], BF16) for i in range(2)]
                        hacc = sb7(f"hacc{s}", [128, TB], F32)
                        gblk = sb7(f"gblk{s}", [128, TBT, D], BF16)
                        oblk_1 = sb7(f"oblk{s}", [128, TBT, D], F32)
                        oblk = [oblk_1, oblk_1]
                        sblk = [sb7(f"sblk{s}{i}", [128, TBT, D], BF16) for i in range(2)]
                        ssqr = sb7(f"ssqr{s}", [1, TB], F32)
                        sqs = sb7(f"sqs{s}", [128, TBT], F32)
                        alpha2 = sb7(f"alpha2{s}", [128, TBT], F32)
                        a2tmp = sb7(f"a2tmp{s}", [128, TBT], F32)
                        wdnc = [sb7(f"wdnc{s}{i}", [128, DBW], BF16) for i in range(6)]
                        p_h1 = [p7.enter_context(nc.psum_tensor(f"ph1{s}{i}", [128, TB], F32))
                                for i in range(2)]
                        p_ssq = p7.enter_context(nc.psum_tensor(f"pssq{s}", [1, TB], F32))
                        p_ssc = p7.enter_context(nc.psum_tensor(f"pssc{s}", [128, TBT], F32))
                        p_dn = [p7.enter_context(nc.psum_tensor(f"pdn{s}{i}", [128, DBW], F32))
                                for i in range(4)]
                        p_xt = p_dn[0]

                        em.wait("q", "gwa", tern_gw[0])
                        em.wait("q", "gwb", tern_gw[1])
                        em.wait("q", "t", em.val("t"))
                        em.wait("q", "v", em.val("v"))
                        em.wait("q", "s", em.val("s"))
                        for dc in range(DC):
                            em.op("q", lambda e, dc=dc: e.dma_start_transpose(
                                out=wupT[:, dc, :], in_=wupt_h[:, dc * 128:(dc + 1) * 128]),
                                inc=qu, incn=16)
                        wupT_ld = em.val(qu)

                        prev = dict(fin=0, dn=0, sil=0, sq=0, ssc=0, sq2=0, ssqmm=0)
                        wdn_guard = [0] * 6
                        sblk_guard = [0, 0]
                        xblk_guard = [0, 0]
                        xnblk_guard = [0, 0]
                        xn_info = {}
                        pend_fin = [None]
                        em.wait("v", "gsb", em.val("gsb"))
                        em.wait("s", "gsb", em.val("gsb"))
                        p7v0, p7t0, p7s0 = em.val("v"), em.val("t"), em.val("s")
                        prev["gelu"] = p7s0
                        GP = DBW // 128
                        NGRP = TBT // GP

                        def emit_loads_xn(blk):
                            t0 = blk * TBT
                            spar = blk % 2
                            em.wait("g", "v", xblk_guard[spar] if blk >= 2 else p7v0)
                            if blk == 0:
                                em.wait("g", "t", p7t0)
                                em.wait("g", "s", p7s0)
                            em.op("g", lambda e, s=s, t0=t0, spar=spar: e.dma_start(
                                out=xblk[spar][:],
                                in_=x_h[s].rearrange("(t p) d -> p t d", p=128)[:, t0:t0 + TBT, :]),
                                inc=["gxa", "gxb"][spar], incn=16)
                            x_ld2 = em.val(["gxa", "gxb"][spar])
                            em.wait("q", "gsb", em.val("gsb"))
                            em.wait("q", "v", sblk_guard[spar])
                            em.op("q", lambda e, s=s, t0=t0, spar=spar: e.dma_start(
                                out=sblk[spar][:],
                                in_=s_h[s].rearrange("(t p) d -> p t d", p=128)[:, t0:t0 + TBT, :]),
                                inc=["qs0", "qs1"][spar], incn=16)
                            s_ld = em.val(["qs0", "qs1"][spar])
                            em.wait("v", ["gxa", "gxb"][spar], x_ld2)
                            em.wait("v", "t", xnblk_guard[spar])
                            for tt in range(TBT):
                                em.drain("v")
                                em.op("v", lambda e, tt=tt, t0=t0, spar=spar: e.tensor_scalar(
                                    xnblk[spar][:, tt, :], xblk[spar][:, tt, :],
                                    stat_nm[:, t0 + tt:t0 + tt + 1],
                                    stat_is[:, t0 + tt:t0 + tt + 1], ALU.add, ALU.mult), inc="v")
                                em.drain("v")
                                em.op("v", lambda e, tt=tt, t0=t0, spar=spar: e.tensor_scalar(
                                    xnblk[spar][:, tt, :], xnblk[spar][:, tt, :],
                                    stat_a1[:, t0 + tt:t0 + tt + 1], None, ALU.mult), inc="v")
                            xn_info[blk] = (em.val("v"), s_ld)

                        emit_loads_xn(0)
                        for blk in range(NB):
                            t0 = blk * TBT
                            spar = blk % 2
                            xnorm_v, s_ld = xn_info.pop(blk)
                            em.wait("t", "v", xnorm_v)
                            em.wait("t", "s", prev["gelu"])
                            pxt_guard = {}
                            for dc in range(DC):
                                for grp in range(NGRP):
                                    bank = (dc * NGRP + grp) % 2
                                    em.wait("t", "v", pxt_guard.get(bank, 0))
                                    for k in range(GP):
                                        tt = grp * GP + k
                                        em.op("t", lambda e, dc=dc, tt=tt, k=k, bank=bank,
                                              spar=spar: e.matmul(
                                                  p_h1[bank][:, k * 128:(k + 1) * 128],
                                                  xnblk[spar][:, tt, dc * 128:(dc + 1) * 128],
                                                  ident[:, 0:128], start=True, stop=True),
                                              inc="t" if k == GP - 1 else None)
                                    em.wait("v", "t", em.val("t"))
                                    em.drain("v")
                                    pxt_guard[bank] = em.op(
                                        "v", lambda e, dc=dc, grp=grp, bank=bank: e.tensor_copy(
                                            xnormT[:, dc, grp * DBW:(grp + 1) * DBW],
                                            p_h1[bank][:, 0:DBW]), inc="v")
                            xnT_v = em.val("v")
                            xnblk_guard[spar] = em.val("t")
                            if pend_fin[0] is not None:
                                pend_fin[0]()
                                pend_fin[0] = None
                            em.wait("t", "v", xnT_v)
                            if blk == 0:
                                em.wait("t", qu, wupT_ld)
                            gelu = [0] * FT
                            for ft in range(FT):
                                par = ft % 2
                                if ft >= 2:
                                    em.wait("t", "s", gelu[ft - 2])
                                for dc in range(DC):
                                    em.op("t", lambda e, ft=ft, dc=dc, par=par: e.matmul(
                                        p_h1[par][:, :], wupT[:, dc, ft * 128:(ft + 1) * 128],
                                        xnormT[:, dc, :], start=(dc == 0), stop=(dc == DC - 1)),
                                        inc="t" if dc == DC - 1 else None)
                                em.wait("s", "t", em.val("t"))
                                if ft == 0:
                                    em.wait("s", "t", prev["dn"])
                                em.drain("s")
                                gelu[ft] = em.op("s", lambda e, ft=ft, par=par: e.activation(
                                    h1gT[:, ft, :], p_h1[par][:, :], AF.Gelu), inc="s")
                                em.wait("v", "s", gelu[ft])
                                if ft == 0:
                                    em.wait("v", "t", prev["ssqmm"])
                                em.drain("v")
                                em.op("v", lambda e, ft=ft, par=par: e.tensor_tensor(
                                    hsq[par][:], h1gT[:, ft, :], h1gT[:, ft, :], ALU.mult),
                                    inc="v")
                                em.drain("v")
                                if ft == 0:
                                    em.op("v", lambda e: e.tensor_copy(hacc[:], hsq[0][:]),
                                          inc="v")
                                else:
                                    em.op("v", lambda e, par=par: e.tensor_tensor(
                                        hacc[:], hacc[:], hsq[par][:], ALU.add), inc="v")
                                em.drain("v")
                            gelu_all = gelu[FT - 1]
                            prev["gelu"] = gelu_all
                            em.op("v", lambda e: e.tensor_copy(hsq[0][:], hacc[:]), inc="v")
                            hacc_v = em.val("v")
                            em.wait("t", "v", hacc_v)
                            em.wait("t", "v", prev["sq"])
                            em.op("t", lambda e: e.matmul(
                                p_ssq[0:1, :], ones_cb[:, 0:1], hsq[0][:],
                                start=True, stop=True), inc="t")
                            prev["ssqmm"] = ssq_t = em.val("t")
                            sil_s = prev["sil"]

                            def emit_down_mms(db, sil_guard):
                                for fc in range(FT):
                                    wpar = fc % 6
                                    if s + blk + db + fc == 0:
                                        em.wait("q", "gwa", tern_gw[0])
                                        em.wait("q", "gwb", tern_gw[1])
                                    em.wait("q", "t", wdn_guard[wpar])
                                    em.op("q", lambda e, fc=fc, db=db, wpar=wpar:
                                          e.dma_start_transpose(
                                              out=wdnc[wpar][:],
                                              in_=wdnt_h[db * DBW:(db + 1) * DBW,
                                                         fc * 128:(fc + 1) * 128]),
                                          inc=f"qd{wpar}", incn=16)
                                    em.wait("t", f"qd{wpar}", em.val(f"qd{wpar}"))
                                    if fc == 0:
                                        em.wait("t", "s", gelu_all)
                                        em.wait("t", "s", sil_guard)
                                    for tt in range(TBT):
                                        em.op("t", lambda e, fc=fc, tt=tt, wpar=wpar: e.matmul(
                                            p_dn[tt][:, :], h1gT[:, fc, tt * 128:(tt + 1) * 128],
                                            wdnc[wpar][:, :], start=(fc == 0), stop=(fc == FT - 1)),
                                            inc="t" if tt == TBT - 1 else None)
                                    wdn_guard[wpar] = em.val("t")
                                return em.val("t")

                            dn_t0 = emit_down_mms(0, sil_s)
                            em.wait("v", "t", ssq_t)
                            em.wait("v", "t", prev["ssc"])
                            em.drain("v")
                            prev["sq"] = em.op(
                                "v", lambda e: e.tensor_copy(ssqr[0:1, :], p_ssq[0:1, :]), inc="v")
                            em.wait("t", "v", prev["sq"])
                            em.wait("t", "s", prev["sq2"])
                            for tt in range(TBT):
                                em.op("t", lambda e, tt=tt: e.matmul(
                                    p_ssc[:, tt:tt + 1], ssqr[0:1, tt * 128:(tt + 1) * 128],
                                    one11[0:1, 0:1], start=True, stop=True),
                                    inc="t" if tt == TBT - 1 else None)
                            prev["ssc"] = em.val("t")
                            em.wait("s", "t", prev["ssc"])
                            em.drain("s")
                            prev["sq2"] = em.op(
                                "s", lambda e: e.activation(sqs[:, :], p_ssc[:, :], AF.Sqrt), inc="s")
                            em.wait("v", "s", prev["sq2"])
                            em.wait("v", "s", prev["sil"])
                            em.drain("v")
                            em.op("v", lambda e: e.tensor_scalar(
                                a2tmp[:, :], sqs[:, :], 1.0 / float(np.sqrt(DFF)), 1e-8,
                                ALU.mult, ALU.add), inc="v")
                            em.drain("v")
                            em.op("v", lambda e: e.reciprocal(alpha2[:, :], a2tmp[:, :]), inc="v")
                            a2_v = em.val("v")
                            if blk + 1 < NB:
                                emit_loads_xn(blk + 1)
                            em.wait("s", "t", dn_t0)
                            em.wait("s", "v", a2_v)
                            em.wait("s", "v", prev["fin"])
                            for tt in range(TBT):
                                em.drain("s")
                                em.op("s", lambda e, tt=tt: e.activation(
                                    gblk[:, tt, 0:DBW], p_dn[tt][:, :],
                                    AF.Silu, scale=alpha2[:, tt:tt + 1]), inc="s")
                            sil_s = em.val("s")
                            dn_t1 = emit_down_mms(1, sil_s)
                            em.wait("s", "t", dn_t1)
                            for tt in range(TBT):
                                em.drain("s")
                                em.op("s", lambda e, tt=tt: e.activation(
                                    gblk[:, tt, DBW:2 * DBW], p_dn[tt][:, :],
                                    AF.Silu, scale=alpha2[:, tt:tt + 1]), inc="s")
                            sil_s = em.val("s")
                            prev["dn"] = em.val("t")
                            prev["sil"] = sil_s

                            def fin_fn(blk=blk, spar=spar, s_ld=s_ld, sil_s=sil_s, t0=t0):
                                em.wait("v", "s", sil_s)
                                em.wait("v", ["qs0", "qs1"][spar], s_ld)
                                em.wait("v", "goa", em.val("goa"))
                                em.wait("v", "gob", em.val("gob"))
                                for tt in range(TBT):
                                    em.drain("v")
                                    em.op("v", lambda e, tt=tt, spar=spar: e.tensor_tensor(
                                        oblk[spar][:, tt, :], sblk[spar][:, tt, :], gblk[:, tt, :],
                                        ALU.mult), inc="v")
                                    em.drain("v")
                                    em.op("v", lambda e, tt=tt, spar=spar: e.tensor_tensor(
                                        oblk[spar][:, tt, :], oblk[spar][:, tt, :],
                                        xblk[spar][:, tt, :], ALU.add), inc="v")
                                fin_v = em.val("v")
                                sblk_guard[spar] = fin_v
                                xblk_guard[spar] = fin_v
                                prev["fin"] = fin_v
                                em.wait("g", "v", fin_v)
                                em.op("g", lambda e, s=s, t0=t0, spar=spar: e.dma_start(
                                    out=out_h[s].rearrange("(t p) d -> p t d", p=128)[
                                        :, t0:t0 + TBT, :],
                                    in_=oblk[spar][:]), inc=["goa", "gob"][spar], incn=16)
                            pend_fin[0] = fin_fn
                        if pend_fin[0] is not None:
                            pend_fin[0]()
                            pend_fin[0] = None

            _do_seq()
        em.wait("g", "goa", em.val("goa"))
        em.wait("g", "gob", em.val("gob"))

        with nc.Block() as block:
            @block.tensor
            def _(eng):
                for f in em.ops["t"]:
                    f(eng)

            @block.scalar
            def _(eng):
                for f in em.ops["s"]:
                    f(eng)

            @block.vector
            def _(eng):
                for f in em.ops["v"]:
                    f(eng)

            @block.sync
            def _(eng):
                for f in em.ops["q"]:
                    f(eng)

            @block.gpsimd
            def _(eng):
                for f in em.ops["g"]:
                    f(eng)

    return nc


def kernel(L0_attn_x, L0_attn_w_qkv, L0_attn_w_out, L0_ffn_w_up, L0_ffn_w_down):
    x = np.ascontiguousarray(L0_attn_x, dtype=np.float32)
    wqkv = np.ascontiguousarray(L0_attn_w_qkv, dtype=np.float32)
    wout = np.ascontiguousarray(L0_attn_w_out, dtype=np.float32)
    wup = np.ascontiguousarray(L0_ffn_w_up, dtype=np.float32)
    wdn = np.ascontiguousarray(L0_ffn_w_down, dtype=np.float32)
    ident = np.eye(128, dtype=ml_dtypes.bfloat16)
    n = 8
    nc = build(FULL)
    in_maps = [{
        "x": x[2 * c:2 * c + 2],
        "w_qkv": wqkv, "w_out": wout, "w_up": wup, "w_down": wdn,
        "ident": ident,
    } for c in range(n)]
    res = run_bass_kernel_spmd(nc, in_maps, core_ids=list(range(n)))
    return np.concatenate([r["out"] for r in res.results], axis=0)


# version 18
# speedup vs baseline: 1.0639x; 1.0639x over previous
import numpy as np
import ml_dtypes
from contextlib import ExitStack

import concourse.bass as bass
import concourse.mybir as mybir
from concourse.bass_utils import run_bass_kernel_spmd

F32 = mybir.dt.float32
BF16 = mybir.dt.bfloat16
ALU = mybir.AluOpType
AF = mybir.ActivationFunctionType
AX = mybir.AxisListType.X

FULL = dict(B=2, S=2048, D=1024, H=16, DFF=4096)


class Em:
    ENGINES = ("t", "s", "v", "q", "g")

    def __init__(self):
        self.ops = {e: [] for e in self.ENGINES}
        self.cnt = {}
        self.sems = {}

    def op(self, e, fn, inc=None, incn=1):
        v = None
        if inc is not None:
            v = self.cnt.get(inc, 0) + incn
            self.cnt[inc] = v

        def run(eng, fn=fn, inc=inc, incn=incn):
            r = fn(eng)
            if inc is not None:
                r.then_inc(self.sems[inc], incn)

        self.ops[e].append(run)
        return v

    def wait(self, e, sem, v):
        assert v <= self.cnt.get(sem, 0), f"wait {sem}>={v}, only {self.cnt.get(sem, 0)} planned"
        if v <= 0:
            return
        self.ops[e].append(lambda eng, sem=sem, v=v: eng.wait_ge(self.sems[sem], v))

    def drain(self, e):
        self.ops[e].append(lambda eng: eng.drain())

    def val(self, sem):
        return self.cnt.get(sem, 0)


def build(cfg):
    B, S, D, H, DFF = cfg["B"], cfg["S"], cfg["D"], cfg["H"], cfg["DFF"]
    DH, WIN, DIL = 64, 128, 4
    DC = D // 128
    HP = H // 2
    NT = S // 128
    ND = S // DIL
    NJD = ND // 128
    NW = S // WIN
    SH = S // 2
    NTH = NT // 2
    NWH = NW // 2
    NIS_D = (ND // 2) // 128
    TB = 512
    NB = S // TB
    TBT = TB // 128
    FT = DFF // 128
    DBW = min(512, D)
    NDB = D // DBW
    SB512 = S // 512
    assert H * DH == D and NWH % 4 == 0 and SH % 512 == 0 and ND % 256 == 0

    nc = bass.Bass("TRN2", target_bir_lowering=False)
    x_h = nc.dram_tensor("x", [B, S, D], F32, kind="ExternalInput").ap()
    wqkv_h = nc.dram_tensor("w_qkv", [D, 3 * D], F32, kind="ExternalInput").ap()
    wout_h = nc.dram_tensor("w_out", [D, D], F32, kind="ExternalInput").ap()
    wup_h = nc.dram_tensor("w_up", [DFF, D], F32, kind="ExternalInput").ap()
    wdn_h = nc.dram_tensor("w_down", [D, DFF], F32, kind="ExternalInput").ap()
    ident_h = nc.dram_tensor("ident", [128, 128], BF16, kind="ExternalInput").ap()
    out_h = nc.dram_tensor("out", [B, S, D], F32, kind="ExternalOutput").ap()
    xn_h = nc.dram_tensor("xn_scr", [B, S, D], BF16, kind="Internal").ap()
    s_h = nc.dram_tensor("s_scr", [B, S, D], BF16, kind="Internal").ap()
    wupt_h = nc.dram_tensor("wupt_scr", [DFF, D], BF16, kind="Internal").ap()
    wdnt_h = nc.dram_tensor("wdnt_scr", [D, DFF], BF16, kind="Internal").ap()

    em = Em()
    st = ExitStack()
    with st:
        sb = lambda name, shape, dt: st.enter_context(nc.sbuf_tensor(name, shape, dt))
        for sn in ("t", "s", "v", "qa", "qw0", "qw1", "qx0", "qx1", "qt0", "qt1",
                   "qu0", "qu1", "qd0", "qd1", "qd2", "qd3", "qd4", "qd5", "qs0", "qs1",
                   "gwa", "gwb", "gna", "gnb", "gsb", "gxa", "gxb", "goa", "gob"):
            em.sems[sn] = st.enter_context(nc.semaphore(sn))

        ident = sb("ident_sb", [128, 128], BF16)
        ones_cb = sb("ones_cb", [128, 1], BF16)
        one11 = sb("one11", [1, 1], F32)
        ones_cf = sb("ones_cf", [128, 1], F32)
        ones_rf = sb("ones_rf", [1, 128], F32)
        thrp = sb("thrp", [128, 1], F32)
        thrn = sb("thrn", [128, 1], F32)
        thr11 = sb("thr11", [1, 1], F32)
        stat_nm = sb("stat_nm", [128, NT], F32)
        stat_is = sb("stat_is", [128, NT], F32)
        stat_a1 = sb("stat_a1", [128, NT], F32)
        scr_a = sb("scr_a", [128, NT], F32)
        scr_b = sb("scr_b", [128, NT], F32)
        red1 = sb("red1", [128, 1], F32)
        trash = sb("trash", [128, D], BF16)

        em.op("q", lambda e: e.dma_start(out=ident[:], in_=ident_h[:]), inc="qa", incn=16)
        ld_ident = em.val("qa")
        em.op("v", lambda e: e.memset(ones_cb[:], 1.0), inc="v")
        em.op("v", lambda e: e.memset(one11[0:1, :], 1.0), inc="v")
        em.op("v", lambda e: e.memset(ones_cf[:], 1.0), inc="v")
        em.op("v", lambda e: e.memset(ones_rf[0:1, :], 1.0), inc="v")

        CMAX = max(D, DFF)
        with ExitStack() as p2:
            sb2 = lambda name, shape, dt: p2.enter_context(nc.sbuf_tensor(name, shape, dt))
            wch = [sb2(f"wch{i}", [128, CMAX], F32) for i in range(2)]
            btrash = sb2("btrash", [128, CMAX], BF16)
            wtc = [sb2(f"wtc{i}", [128, CMAX], BF16) for i in range(2)]
            ncmp = sb2("ncmp", [128, CMAX], F32)
            asums = sb2("asums", [128, 32], F32)
            p_red = p2.enter_context(nc.psum_tensor("p_red", [128, 2], F32))

            wch_guard = [("s", 0), ("s", 0)]
            wtc_guard = [0, 0]
            qsem = ["qw0", "qw1"]
            for wi, (w_src, wt_dst, R, C) in enumerate(
                    [(wup_h, wupt_h, DFF, D), (wdn_h, wdnt_h, D, DFF)]):
                NCH = R // 128
                for c in range(NCH):
                    par = c % 2
                    geng, gval = wch_guard[par]
                    em.wait("q", geng, gval)
                    em.op("q", lambda e, c=c, par=par, w_src=w_src, C=C: e.dma_start(
                        out=wch[par][:, 0:C], in_=w_src[c * 128:(c + 1) * 128, :]),
                        inc=qsem[par], incn=16)
                    em.wait("s", qsem[par], em.val(qsem[par]))
                    em.drain("s")
                    wch_guard[par] = ("s", em.op(
                        "s", lambda e, c=c, par=par, C=C: e.activation(
                            btrash[:, 0:C], wch[par][:, 0:C], AF.Abs,
                            accum_out=asums[:, c:c + 1]), inc="s"))
                em.wait("v", "s", em.val("s"))
                em.drain("v")
                em.op("v", lambda e, NCH=NCH: e.reduce_sum(red1[:, 0:1], asums[:, 0:NCH], AX),
                      inc="v")
                em.wait("t", "v", em.val("v"))
                em.wait("t", "qa", ld_ident)
                em.op("t", lambda e: e.matmul(p_red[0:1, 0:1], ones_cf[:, 0:1], red1[:, 0:1],
                                              start=True, stop=True), inc="t")
                em.wait("v", "t", em.val("t"))
                em.op("v", lambda e, R=R, C=C: e.tensor_scalar(
                    thr11[0:1, 0:1], p_red[0:1, 0:1], 0.5 / (R * C), None, ALU.mult), inc="v")
                em.wait("t", "v", em.val("v"))
                em.op("t", lambda e: e.matmul(p_red[:, 1:2], ones_rf[0:1, :], thr11[0:1, 0:1],
                                              start=True, stop=True), inc="t")
                em.wait("v", "t", em.val("t"))
                em.op("v", lambda e: e.tensor_copy(thrp[:, 0:1], p_red[:, 1:2]), inc="v")
                em.drain("v")
                em.op("v", lambda e: e.tensor_scalar(thrn[:, 0:1], thrp[:, 0:1], -1.0, None,
                                                     ALU.mult), inc="v")
                em.drain("v")
                for c in range(NCH):
                    par = c % 2
                    geng, gval = wch_guard[par]
                    em.wait("q", geng, gval)
                    em.op("q", lambda e, c=c, par=par, w_src=w_src, C=C: e.dma_start(
                        out=wch[par][:, 0:C], in_=w_src[c * 128:(c + 1) * 128, :]),
                        inc=qsem[par], incn=16)
                    em.wait("v", qsem[par], em.val(qsem[par]))
                    em.drain("v")
                    em.op("v", lambda e, par=par, C=C: e.tensor_single_scalar(
                        ncmp[:, 0:C], wch[par][:, 0:C], thrn[:, 0:1], ALU.is_lt), inc="v")
                    em.drain("v")
                    em.wait("v", ["gwa", "gwb"][par], wtc_guard[par])
                    wch_guard[par] = ("v", em.op(
                        "v", lambda e, par=par, C=C: e.scalar_tensor_tensor(
                            wtc[par][:, 0:C], wch[par][:, 0:C], thrp[:, 0:1], ncmp[:, 0:C],
                            ALU.is_gt, ALU.subtract), inc="v"))
                    em.wait("g", "v", wch_guard[par][1])
                    wtc_guard[par] = em.op(
                        "g", lambda e, c=c, par=par, wt_dst=wt_dst, C=C: e.dma_start(
                            out=wt_dst[c * 128:(c + 1) * 128, :], in_=wtc[par][:, 0:C]),
                        inc=["gwa", "gwb"][par], incn=16)
            tern_gw = (em.val("gwa"), em.val("gwb"))

        for s in range(B):
            def _do_seq(s=s):
                qx, qt, qu = f"qx{s}", f"qt{s}", f"qu{s}"
                with ExitStack() as sq:
                    sbq = lambda name, shape, dt: sq.enter_context(nc.sbuf_tensor(name, shape, dt))
                    with ExitStack() as p3:
                        xs = p3.enter_context(nc.sbuf_tensor(f"xs{s}", [128, NT, D], F32))
                        xnb = [p3.enter_context(nc.sbuf_tensor(f"xnb{s}{i}", [128, D], BF16))
                               for i in range(2)]
                        for gsem_ in ("v", "s", "t", "gwa", "gwb", "gna", "gnb", "gsb", "goa", "gob"):
                            em.wait("q", gsem_, em.val(gsem_))
                        for i in range(4):
                            em.op("q", lambda e, s=s, i=i: e.dma_start(
                                out=xs[:, i * (NT // 4):(i + 1) * (NT // 4), :],
                                in_=x_h[s].rearrange("(t p) d -> p t d", p=128)[
                                    :, i * (NT // 4):(i + 1) * (NT // 4), :]), inc=qx, incn=16)
                        x_ld = em.val(qx)
                        em.wait("s", qx, x_ld)
                        em.wait("v", qx, x_ld)
                        for tt in range(NT):
                            em.drain("s")
                            em.op("s", lambda e, tt=tt: e.activation(
                                trash[:, 0:D], xs[:, tt, :], AF.Square,
                                accum_out=scr_a[:, tt:tt + 1]), inc="s")
                            em.op("v", lambda e, tt=tt: e.reduce_sum(
                                scr_b[:, tt:tt + 1], xs[:, tt, :], AX), inc="v")
                            em.drain("v")
                        em.drain("v")
                        em.op("v", lambda e: e.tensor_scalar(stat_nm[:], scr_b[:], -1.0 / D, None,
                                                             ALU.mult), inc="v")
                        em.wait("v", "s", em.val("s"))
                        em.drain("v")
                        em.op("v", lambda e: e.tensor_scalar(scr_a[:], scr_a[:], 1.0 / D, None,
                                                             ALU.mult), inc="v")
                        em.drain("v")
                        em.op("v", lambda e: e.tensor_tensor(scr_b[:], stat_nm[:], stat_nm[:],
                                                             ALU.mult), inc="v")
                        em.drain("v")
                        em.op("v", lambda e: e.tensor_tensor(scr_a[:], scr_a[:], scr_b[:],
                                                             ALU.subtract), inc="v")
                        em.drain("v")
                        em.op("v", lambda e: e.tensor_scalar(scr_b[:], scr_a[:], 1e-5, None,
                                                             ALU.add), inc="v")
                        em.drain("v")
                        em.op("v", lambda e: e.reciprocal(scr_b[:], scr_b[:]), inc="v")
                        em.drain("v")
                        em.op("v", lambda e: e.tensor_tensor(scr_a[:], scr_a[:], scr_b[:],
                                                             ALU.mult), inc="v")
                        em.drain("v")
                        vst = em.val("v")
                        em.wait("s", "v", vst)
                        em.drain("s")
                        em.op("s", lambda e: e.activation(stat_is[:], scr_b[:], AF.Sqrt), inc="s")
                        em.drain("s")
                        em.op("s", lambda e: e.activation(scr_a[:], scr_a[:], AF.Sqrt), inc="s")
                        em.wait("v", "s", em.val("s"))
                        em.drain("v")
                        em.op("v", lambda e: e.tensor_scalar(scr_a[:], scr_a[:], 1e-8, None,
                                                             ALU.add), inc="v")
                        em.drain("v")
                        em.op("v", lambda e: e.reciprocal(stat_a1[:], scr_a[:]), inc="v")
                        em.drain("v")
                        em.wait("v", "gwa", tern_gw[0])
                        em.wait("v", "gwb", tern_gw[1])
                        em.wait("v", "go", em.val("go"))
                        em.wait("v", "t", em.val("t"))
                        for tt in range(NT):
                            par = tt % 2
                            gsem = ["gna", "gnb"][par]
                            if tt >= 2:
                                em.wait("v", gsem, em.val(gsem))
                            em.drain("v")
                            em.op("v", lambda e, tt=tt, par=par: e.tensor_scalar(
                                xnb[par][:], xs[:, tt, :], stat_nm[:, tt:tt + 1],
                                stat_is[:, tt:tt + 1], ALU.add, ALU.mult), inc="v")
                            em.wait("g", "v", em.val("v"))
                            em.op("g", lambda e, s=s, tt=tt, par=par, gsem=gsem: e.dma_start(
                                out=xn_h[s].rearrange("(t p) d -> p t d", p=128)[:, tt, :],
                                in_=xnb[par][:]), inc=gsem, incn=16)
                        xn_done = (em.val("gna"), em.val("gnb"))

                    pa = ExitStack()
                    sba = lambda name, shape, dt: pa.enter_context(nc.sbuf_tensor(name, shape, dt))
                    qT = sba(f"qT{s}", [128, HP, S], BF16)
                    kT = sba(f"kT{s}", [128, HP, S], BF16)
                    v_c = sba(f"v_c{s}", [128, NT, H * 65], BF16)
                    v_d = sba(f"v_d{s}", [128, NT, H * 65], BF16)
                    for gsem_ in ("t", "s", "gna", "gnb", "gwa", "gwb", "gsb", "go"):
                        em.wait("v", gsem_, em.val(gsem_))
                    em.op("v", lambda e: e.memset(v_c[:], 3.0), inc="v")
                    em.op("v", lambda e: e.memset(v_d[:], 3.0), inc="v")
                    vone_v = em.val("v")
                    with ExitStack() as p4:
                        xnT = p4.enter_context(nc.sbuf_tensor(f"xnT{s}", [128, DC, S], BF16))
                        wsl = [p4.enter_context(nc.sbuf_tensor(f"wsl{s}{i}", [128, D], F32))
                               for i in range(2)]
                        wqs = [p4.enter_context(nc.sbuf_tensor(f"wqs{s}{i}", [128, DC, D], BF16))
                               for i in range(2)]
                        p_qkv = [p4.enter_context(nc.psum_tensor(f"pqkv{s}{i}", [128, 512], F32))
                                 for i in range(2)]
                        em.wait("q", "gna", xn_done[0])
                        em.wait("q", "gnb", xn_done[1])
                        for dc in range(DC):
                            em.op("q", lambda e, s=s, dc=dc: e.dma_start_transpose(
                                out=xnT[:, dc, :], in_=xn_h[s][:, dc * 128:(dc + 1) * 128]),
                                inc=qt, incn=16)
                        xnT_ld = em.val(qt)

                        cast_v = [0, 0]
                        evac_eng = [0]

                        def qkv_evac(out_f, in_f):
                            eng = "v" if evac_eng[0] % 2 == 0 else "s"
                            evac_eng[0] += 1
                            em.wait(eng, "t", em.val("t"))
                            em.drain(eng)
                            if eng == "v":
                                cnt = em.op("v", lambda e, o=out_f, i=in_f: e.tensor_copy(o(), i()),
                                            inc="v")
                            else:
                                cnt = em.op("s", lambda e, o=out_f, i=in_f: e.copy(o(), i()),
                                            inc="s")
                            return (eng, cnt)

                        def load_wslice(foff, slot, t_guard):
                            for dc in range(DC):
                                par = dc % 2
                                em.wait("q", "t", t_guard)
                                em.wait("q", "v", cast_v[par])
                                em.op("q", lambda e, dc=dc, par=par, foff=foff: e.dma_start(
                                    out=wsl[par][:], in_=wqkv_h[dc * 128:(dc + 1) * 128,
                                                               foff:foff + D]),
                                    inc=qsem[par], incn=16)
                                em.wait("v", qsem[par], em.val(qsem[par]))
                                cast_v[par] = em.op(
                                    "v", lambda e, dc=dc, par=par, slot=slot: e.tensor_copy(
                                        wqs[slot][:, dc, :], wsl[par][:]), inc="v")
                            return em.val("v")

                        qsem = ["qw0", "qw1"]
                        pq_free = [("v", 0), ("v", 0)]
                        em.wait("t", qt, xnT_ld)
                        mmc = 0
                        cast_done = [0, 0]
                        cast_done[0] = load_wslice(0, 0, em.val("t"))
                        cast_done[1] = load_wslice(D, 1, em.val("t"))
                        for qk in range(2):
                            em.wait("t", "v", cast_done[qk])
                            dst = qT if qk == 0 else kT
                            for c in range(HP):
                                for tb in range(SB512):
                                    par = mmc % 2
                                    mmc += 1
                                    em.wait("t", *pq_free[par])
                                    for dc in range(DC):
                                        em.op("t", lambda e, c=c, tb=tb, dc=dc, par=par, qk=qk:
                                              e.matmul(
                                                  p_qkv[par][:, :],
                                                  wqs[qk][:, dc, c * 128:(c + 1) * 128],
                                                  xnT[:, dc, tb * 512:(tb + 1) * 512],
                                                  start=(dc == 0), stop=(dc == DC - 1)),
                                              inc="t" if dc == DC - 1 else None)
                                    pq_free[par] = qkv_evac(
                                        lambda c=c, tb=tb, dst=dst: dst[:, c, tb * 512:(tb + 1) * 512],
                                        lambda par=par: p_qkv[par][:, :])
                            if qk == 0:
                                q_mms_done = em.val("t")
                        vcast = load_wslice(2 * D, 0, q_mms_done)
                        em.wait("t", "v", vcast)
                        em.wait("t", "v", vone_v)
                        for which, dstv in ((0, v_c), (1, v_d)):
                            for tt in range(NT):
                                for fb in range(NDB):
                                    par = mmc % 2
                                    mmc += 1
                                    em.wait("t", *pq_free[par])
                                    for dc in range(DC):
                                        if which == 0:
                                            lhs_ap = (lambda dc=dc, tt=tt:
                                                      xnT[:, dc, tt * 128:(tt + 1) * 128])
                                        else:
                                            g0_, j0_ = (tt * 128) // ND, (tt * 128) % ND
                                            lhs_ap = (lambda dc=dc, g0_=g0_, j0_=j0_: xnT[
                                                :, dc, g0_ + DIL * j0_: g0_ + DIL * (j0_ + 127) + 1: DIL])
                                        em.op("t", lambda e, lhs_ap=lhs_ap, fb=fb, dc=dc, par=par:
                                              e.matmul(p_qkv[par][:, 0:DBW], lhs_ap(),
                                                       wqs[0][:, dc, fb * DBW:(fb + 1) * DBW],
                                                       start=(dc == 0), stop=(dc == DC - 1)),
                                              inc="t" if dc == DC - 1 else None)
                                    nh = DBW // 64
                                    pq_free[par] = qkv_evac(
                                        lambda tt=tt, fb=fb, dstv=dstv, nh=nh:
                                        dstv[:, tt].rearrange("p (h c) -> p h c", c=65)[
                                            :, fb * nh:(fb + 1) * nh, 0:64],
                                        lambda par=par: p_qkv[par][:, 0:DBW].rearrange(
                                            "p (h c) -> p h c", c=64))
                    qkv_t = em.val("t")
                    qkv_done = (em.val("v"), em.val("s"))

                    s_half = sba(f"s_half{s}", [128, NTH, D], BF16)
                    woutb = sba(f"woutb{s}", [128, DC, D], BF16)
                    wslk = [sba(f"wslk{s}{i}", [128, D], F32) for i in range(2)]
                    wob_v = [0, 0]
                    em.wait("q", "t", qkv_t)
                    em.wait("q", "v", em.val("v"))
                    for dc in range(DC):
                        par = dc % 2
                        em.wait("q", "v", wob_v[par])
                        em.op("q", lambda e, dc=dc, par=par: e.dma_start(
                            out=wslk[par][:], in_=wout_h[dc * 128:(dc + 1) * 128, :]),
                            inc=["qw0", "qw1"][par], incn=16)
                        em.wait("v", ["qw0", "qw1"][par], em.val(["qw0", "qw1"][par]))
                        wob_v[par] = em.op("v", lambda e, dc=dc, par=par: e.tensor_copy(
                            woutb[:, dc, :], wslk[par][:]), inc="v")
                    wout_cast = em.val("v")

                    for half in range(2):
                        def _do_half(half=half):
                            with ExitStack() as ph:
                                onormT = ph.enter_context(
                                    nc.sbuf_tensor(f"onT{s}{half}", [128, DC, SH], BF16))
                                with ExitStack() as p5:
                                    etb = [p5.enter_context(
                                        nc.sbuf_tensor(f"et{s}{half}{i}", [128, 512], BF16))
                                        for i in range(4)]
                                    otr = [p5.enter_context(
                                        nc.sbuf_tensor(f"otr{s}{half}{i}", [65, 512], BF16))
                                        for i in range(4)]
                                    onrm = [p5.enter_context(
                                        nc.sbuf_tensor(f"onrm{s}{half}{i}", [128, 256], BF16))
                                        for i in range(2)]
                                    rec = [p5.enter_context(
                                        nc.sbuf_tensor(f"rec{s}{half}{i}", [128, 4], F32))
                                        for i in range(2)]
                                    p_sc = [p5.enter_context(
                                        nc.psum_tensor(f"psc{s}{half}{h}{i}", [128, 512], F32))
                                        for h in range(2) for i in range(2)]
                                    p_oT = [p5.enter_context(
                                        nc.psum_tensor(f"poT{s}{half}{i}", [65, 512], F32))
                                        for i in range(2)]
                                    p_tr = p5.enter_context(
                                        nc.psum_tensor(f"ptr{s}{half}", [128, 260], F32))
                                    p_bk = p5.enter_context(
                                        nc.psum_tensor(f"pbk{s}{half}", [128, 512], F32))

                                    gd = dict(psc=[[0, 0], [0, 0]], evac=[0, 0],
                                              otr_g=[[0, 0], [0, 0]],
                                              onrm=[0, 0], ptr=0, pbk=0, av=[0, 0])
                                    pend = [None, None]
                                    ucnt = [0]

                                    def emit_evacs(NIS, upar):
                                        NIC = NIS * 128
                                        ev = [0, 0]
                                        for hi in range(2):
                                            em.wait("s", "t", gd["av"][hi])
                                            em.wait("s", "t", gd["otr_g"][hi][upar])
                                            em.drain("s")
                                            ev[hi] = em.op(
                                                "s", lambda e, hi=hi, NIC=NIC, upar=upar: e.copy(
                                                    otr[hi * 2 + upar][0:65, 0:NIC],
                                                    p_oT[hi][0:65, 0:NIC]), inc="s")
                                        gd["evac"] = ev[:]
                                        return ev

                                    def tail_fwd_one(NIS, upar, ev, hi, prev_ts):
                                        em.wait("t", "s", ev[hi])
                                        em.wait("t", "v", gd["ptr"] if hi == 0 else prev_ts)
                                        for k in range(NIS):
                                            em.op("t", lambda e, hi=hi, k=k, upar=upar: e.matmul(
                                                p_tr[:, k * 65:(k + 1) * 65],
                                                otr[hi * 2 + upar][0:65, k * 128:(k + 1) * 128],
                                                ident[0:65, 0:65], start=True, stop=True),
                                                inc="t" if k == NIS - 1 else None)
                                        gd["otr_g"][hi][upar] = em.val("t")
                                        em.wait("v", "t", em.val("t"))
                                        em.wait("v", "t", gd["onrm"][hi])
                                        em.drain("v")
                                        em.op("v", lambda e, hi=hi, NIS=NIS: e.reciprocal(
                                            rec[hi][:, 0:NIS], p_tr[:, 64:NIS * 65:65]), inc="v")
                                        em.drain("v")
                                        for k in range(NIS):
                                            em.op("v", lambda e, hi=hi, k=k: e.tensor_scalar(
                                                onrm[hi][:, k * 64:(k + 1) * 64],
                                                p_tr[:, k * 65:k * 65 + 64],
                                                rec[hi][:, k:k + 1], None, ALU.mult), inc="v")
                                            em.drain("v")
                                        return em.val("v")

                                    def tail_rest(NIS, merge, ts_done):
                                        em.wait("t", "v", ts_done[1])
                                        em.wait("t", "v", gd["pbk"])
                                        for hi in range(2):
                                            for k in range(NIS):
                                                em.op("t", lambda e, hi=hi, k=k: e.matmul(
                                                    p_bk[hi * 64:(hi + 1) * 64, k * 128:(k + 1) * 128],
                                                    onrm[hi][:, k * 64:(k + 1) * 64], ident[:, 0:128],
                                                    start=True, stop=True, tile_position=(0, hi * 64)),
                                                    inc="t" if (hi == 1 and k == NIS - 1) else None)
                                        bk_t = em.val("t")
                                        gd["onrm"] = [bk_t, bk_t]
                                        em.wait("v", "t", bk_t)
                                        em.drain("v")
                                        merge()
                                        gd["pbk"] = em.val("v")

                                    def run_pend_fwd():
                                        if pend[0] is not None:
                                            NIS, merge, upar, ev, st0 = pend[0]
                                            if st0 is None:
                                                ts0 = tail_fwd_one(NIS, upar, ev, 0, 0)
                                                pend[0] = (NIS, merge, upar, ev, ts0)
                                            else:
                                                ts1 = tail_fwd_one(NIS, upar, ev, 1, st0)
                                                gd["ptr"] = ts1
                                                pend[0] = None
                                                pend[1] = (NIS, merge, (st0, ts1))

                                    def run_pend_rest():
                                        run_pend_fwd()
                                        run_pend_fwd()
                                        if pend[1] is not None:
                                            NIS, merge, ts_done = pend[1]
                                            pend[1] = None
                                            tail_rest(NIS, merge, ts_done)

                                    def finish_unit(NIS, merge):
                                        upar = ucnt[0] % 2
                                        ucnt[0] += 1
                                        run_pend_rest()
                                        ev = emit_evacs(NIS, upar)
                                        pend[0] = (NIS, merge, upar, ev, None)

                                    def score_av_loop(hp, jlist, NIS, sc_ap, av_st, locmode=False,
                                                      mid_hook=None):
                                        NIC = NIS * 128
                                        hA, hB = 2 * hp, 2 * hp + 1
                                        NJ = len(jlist)
                                        if locmode:
                                            tsc = [0, 0]
                                            for jn, j in enumerate(jlist):
                                                for hi, h in enumerate((hA, hB)):
                                                    if jn == 0:
                                                        em.wait("t", "s", gd["psc"][hi][0])
                                                    lhsT, rhs, c0 = sc_ap(h, j)
                                                    em.op("t", lambda e, lhsT=lhsT, rhs=rhs, c0=c0, hi=hi:
                                                          e.matmul(p_sc[hi * 2][:, c0:c0 + 128],
                                                                   lhsT(), rhs(),
                                                                   start=True, stop=True), inc="t")
                                                    tsc[hi] = em.val("t")
                                            exps = [0, 0]
                                            for hi in range(2):
                                                em.wait("s", "t", tsc[hi])
                                                em.op("s", lambda e, hi=hi, NIC=NIC: e.activation(
                                                    etb[hi * 2][:, 0:NIC], p_sc[hi * 2][:, 0:NIC],
                                                    AF.Exp, scale=0.125), inc="s")
                                                exps[hi] = em.val("s")
                                                gd["psc"][hi][0] = exps[hi]
                                            if mid_hook is not None:
                                                mid_hook()
                                                mid_hook()
                                            for hi, h in enumerate((hA, hB)):
                                                em.wait("t", "s", exps[hi])
                                                em.wait("t", "s", gd["evac"][hi])
                                                for jn, j in enumerate(jlist):
                                                    em.op("t", lambda e, h=h, hi=hi, j=j, jn=jn: e.matmul(
                                                        p_oT[hi][0:65, jn * 128:(jn + 1) * 128],
                                                        av_st(h, j)(),
                                                        etb[hi * 2][:, jn * 128:(jn + 1) * 128],
                                                        start=True, stop=True),
                                                        inc="t" if jn == NJ - 1 else None)
                                                gd["av"][hi] = em.val("t")
                                            return
                                        tsc = [[0] * NJ for _ in range(2)]
                                        exps = [[0] * NJ for _ in range(2)]

                                        def emit_av(jn):
                                            par = jn % 2
                                            for hi, h in enumerate((hA, hB)):
                                                em.wait("t", "s", exps[hi][jn])
                                                if jn == 0:
                                                    em.wait("t", "s", gd["evac"][hi])
                                                em.op("t", lambda e, h=h, hi=hi, j=jlist[jn], par=par,
                                                      jn=jn, NIC=NIC: e.matmul(
                                                          p_oT[hi][0:65, 0:NIC], av_st(h, j)(),
                                                          etb[hi * 2 + par][:, 0:NIC],
                                                          start=(jn == 0), stop=(jn == NJ - 1)),
                                                      inc="t" if jn == NJ - 1 else None)
                                                if jn == NJ - 1:
                                                    gd["av"][hi] = em.val("t")

                                        for jn, j in enumerate(jlist):
                                            par = jn % 2
                                            for hi, h in enumerate((hA, hB)):
                                                if jn >= 2:
                                                    em.wait("t", "s", exps[hi][jn - 2])
                                                else:
                                                    em.wait("t", "s", gd["psc"][hi][par])
                                                lhsT, rhs, _ = sc_ap(h, j)
                                                em.op("t", lambda e, lhsT=lhsT, rhs=rhs, hi=hi, par=par,
                                                      NIC=NIC: e.matmul(
                                                          p_sc[hi * 2 + par][:, 0:NIC], lhsT(), rhs(),
                                                          start=True, stop=True), inc="t")
                                                tsc[hi][jn] = em.val("t")
                                            for hi in range(2):
                                                em.wait("s", "t", tsc[hi][jn])
                                                em.op("s", lambda e, hi=hi, par=par, NIC=NIC: e.activation(
                                                    etb[hi * 2 + par][:, 0:NIC],
                                                    p_sc[hi * 2 + par][:, 0:NIC],
                                                    AF.Exp, scale=0.125), inc="s")
                                                exps[hi][jn] = em.val("s")
                                                gd["psc"][hi][par] = exps[hi][jn]
                                            if jn >= 1:
                                                emit_av(jn - 1)
                                            if mid_hook is not None and (
                                                    jn == min(1, NJ - 2) or jn == min(3, NJ - 1)):
                                                mid_hook()
                                        emit_av(NJ - 1)

                                    hd = lambda h: ((h % 2) * 64, h // 2)

                                    em.wait("t", "v", em.val("v"))
                                    em.wait("t", "s", em.val("s"))
                                    for hp in range(HP):
                                        for u in range(NWH // 4):
                                            w0 = half * NWH + u * 4
                                            wl = list(range(w0, w0 + 4))

                                            def sc_ap(h, w, w0=w0):
                                                p0, c = hd(h)
                                                return (lambda p0=p0, c=c, w=w:
                                                        kT[p0:p0 + 64, c, w * 128:(w + 1) * 128],
                                                        lambda p0=p0, c=c, w=w:
                                                        qT[p0:p0 + 64, c, w * 128:(w + 1) * 128],
                                                        (w - w0) * 128)

                                            def av_st(h, w):
                                                return lambda h=h, w=w: v_c[:, w, h * 65:h * 65 + 65]

                                            score_av_loop(hp, wl, 4, sc_ap, av_st, locmode=True,
                                                          mid_hook=run_pend_fwd)

                                            def merge(hp=hp, w0=w0, half=half):
                                                c0 = (w0 - half * NWH) * 128
                                                em.op("v", lambda e, hp=hp, c0=c0: e.tensor_copy(
                                                    onormT[:, hp, c0:c0 + 512], p_bk[:, 0:512]), inc="v")
                                                em.drain("v")
                                            finish_unit(4, merge)
                                        for u in range(SH // 512):
                                            i0 = half * SH + u * 512
                                            jl = list(range(NT))

                                            def sc_ap(h, j, i0=i0):
                                                p0, c = hd(h)
                                                return (lambda p0=p0, c=c, j=j:
                                                        kT[p0:p0 + 64, c, j * 128:(j + 1) * 128],
                                                        lambda p0=p0, c=c, i0=i0:
                                                        qT[p0:p0 + 64, c, i0:i0 + 512], 0)

                                            def av_st(h, j):
                                                return lambda h=h, j=j: v_c[:, j, h * 65:h * 65 + 65]

                                            score_av_loop(hp, jl, 4, sc_ap, av_st, mid_hook=run_pend_fwd)

                                            def merge(hp=hp, i0=i0, half=half):
                                                c0 = i0 - half * SH
                                                em.op("v", lambda e, hp=hp, c0=c0: e.scalar_tensor_tensor(
                                                    onormT[:, hp, c0:c0 + 512], p_bk[:, 0:512], 1.0,
                                                    onormT[:, hp, c0:c0 + 512], ALU.mult, ALU.add),
                                                    inc="v")
                                                em.drain("v")
                                            finish_unit(4, merge)
                                        for g in range(DIL):
                                            ip0 = half * (ND // 2)
                                            jl = list(range(NJD))

                                            def sc_ap(h, j, g=g, ip0=ip0):
                                                p0, c = hd(h)
                                                return (lambda p0=p0, c=c, j=j, g=g: kT[
                                                    p0:p0 + 64, c,
                                                    g + DIL * (j * 128): g + DIL * (j * 128 + 127) + 1: DIL],
                                                    lambda p0=p0, c=c, g=g, ip0=ip0: qT[
                                                        p0:p0 + 64, c,
                                                        g + DIL * ip0: g + DIL * (ip0 + NIS_D * 128 - 1) + 1: DIL],
                                                    0)

                                            def av_st(h, j, g=g):
                                                return (lambda h=h, j=j, g=g:
                                                        v_d[:, g * NJD + j, h * 65:h * 65 + 65])

                                            score_av_loop(hp, jl, NIS_D, sc_ap, av_st, mid_hook=run_pend_fwd)

                                            def merge(hp=hp, g=g, ip0=ip0, half=half):
                                                for k in range(NIS_D):
                                                    c0 = g + DIL * (ip0 + k * 128) - half * SH
                                                    em.op("v", lambda e, hp=hp, c0=c0, k=k:
                                                          e.scalar_tensor_tensor(
                                                              onormT[:, hp, c0:c0 + DIL * 127 + 1:DIL],
                                                              p_bk[:, k * 128:(k + 1) * 128], 1.0,
                                                              onormT[:, hp, c0:c0 + DIL * 127 + 1:DIL],
                                                              ALU.mult, ALU.add), inc="v")
                                                    em.drain("v")
                                            finish_unit(NIS_D, merge)

                                    run_pend_fwd()
                                    run_pend_rest()
                                attn_v = em.val("v")
                                with ExitStack() as p6:
                                    p_s6 = [p6.enter_context(
                                        nc.psum_tensor(f"ps6{s}{half}{i}", [128, DBW], F32))
                                        for i in range(2)]
                                    em.wait("t", "v", attn_v)
                                    em.wait("t", "s", em.val("s"))
                                    em.wait("t", "v", wout_cast)
                                    if half == 1 or s == 1:
                                        em.wait("v", "gsb", em.val("gsb"))
                                    ps_free = [0, 0]
                                    pc = 0
                                    for tt in range(NTH):
                                        for db in range(NDB):
                                            par = pc % 2
                                            pc += 1
                                            em.wait("t", "v", ps_free[par])
                                            for dc in range(DC):
                                                em.op("t", lambda e, tt=tt, db=db, dc=dc, par=par:
                                                      e.matmul(p_s6[par][:, :],
                                                               onormT[:, dc, tt * 128:(tt + 1) * 128],
                                                               woutb[:, dc, db * DBW:(db + 1) * DBW],
                                                               start=(dc == 0), stop=(dc == DC - 1)),
                                                      inc="t" if dc == DC - 1 else None)
                                            em.wait("v", "t", em.val("t"))
                                            em.drain("v")
                                            ps_free[par] = em.op(
                                                "v", lambda e, tt=tt, db=db, par=par: e.tensor_copy(
                                                    s_half[:, tt, db * DBW:(db + 1) * DBW],
                                                    p_s6[par][:, :]), inc="v")
                                    em.wait("g", "v", em.val("v"))
                                    em.op("g", lambda e, s=s, half=half: e.dma_start(
                                        out=s_h[s].rearrange("(t p) d -> p t d", p=128)[
                                            :, half * NTH:(half + 1) * NTH, :],
                                        in_=s_half[:]), inc="gsb", incn=16)

                        _do_half()
                    pa.close()
                    with ExitStack() as p7:
                        sb7 = lambda name, shape, dt: p7.enter_context(
                            nc.sbuf_tensor(name, shape, dt))
                        wupT = sb7(f"wupT{s}", [128, DC, DFF], BF16)
                        xblk = [sb7(f"xblk{s}{i}", [128, TBT, D], F32) for i in range(2)]
                        xnblk = [sb7(f"xnblk{s}{i}", [128, TBT, D], BF16) for i in range(2)]
                        xnormT = sb7(f"xnormT{s}", [128, DC, TB], BF16)
                        h1gT = sb7(f"h1gT{s}", [128, FT, TB], BF16)
                        hsq = [sb7(f"hsq{s}{i}", [128, TB], BF16) for i in range(2)]
                        hacc = sb7(f"hacc{s}", [128, TB], F32)
                        gblk = sb7(f"gblk{s}", [128, TBT, D], BF16)
                        oblk_1 = sb7(f"oblk{s}", [128, TBT, D], F32)
                        oblk = [oblk_1, oblk_1]
                        sblk = [sb7(f"sblk{s}{i}", [128, TBT, D], BF16) for i in range(2)]
                        ssqr = sb7(f"ssqr{s}", [1, TB], F32)
                        sqs = sb7(f"sqs{s}", [128, TBT], F32)
                        alpha2 = sb7(f"alpha2{s}", [128, TBT], F32)
                        a2tmp = sb7(f"a2tmp{s}", [128, TBT], F32)
                        wdnc = [sb7(f"wdnc{s}{i}", [128, DBW], BF16) for i in range(6)]
                        p_h1 = [p7.enter_context(nc.psum_tensor(f"ph1{s}{i}", [128, TB], F32))
                                for i in range(2)]
                        p_ssq = p7.enter_context(nc.psum_tensor(f"pssq{s}", [1, TB], F32))
                        p_ssc = p7.enter_context(nc.psum_tensor(f"pssc{s}", [128, TBT], F32))
                        p_dn = [p7.enter_context(nc.psum_tensor(f"pdn{s}{i}", [128, DBW], F32))
                                for i in range(4)]
                        p_xt = p_dn[0]

                        em.wait("q", "gwa", tern_gw[0])
                        em.wait("q", "gwb", tern_gw[1])
                        em.wait("q", "t", em.val("t"))
                        em.wait("q", "v", em.val("v"))
                        em.wait("q", "s", em.val("s"))
                        for dc in range(DC):
                            em.op("q", lambda e, dc=dc: e.dma_start_transpose(
                                out=wupT[:, dc, :], in_=wupt_h[:, dc * 128:(dc + 1) * 128]),
                                inc=qu, incn=16)
                        wupT_ld = em.val(qu)

                        prev = dict(fin=0, dn=0, sil=0, sq=0, ssc=0, sq2=0, ssqmm=0)
                        wdn_guard = [0] * 6
                        sblk_guard = [0, 0]
                        xblk_guard = [0, 0]
                        xnblk_guard = [0, 0]
                        xn_info = {}
                        pend_fin = [None]
                        em.wait("v", "gsb", em.val("gsb"))
                        em.wait("s", "gsb", em.val("gsb"))
                        p7v0, p7t0, p7s0 = em.val("v"), em.val("t"), em.val("s")
                        prev["gelu"] = p7s0
                        GP = DBW // 128
                        NGRP = TBT // GP

                        def emit_loads_xn(blk):
                            t0 = blk * TBT
                            spar = blk % 2
                            em.wait("g", "v", xblk_guard[spar] if blk >= 2 else p7v0)
                            if blk == 0:
                                em.wait("g", "t", p7t0)
                                em.wait("g", "s", p7s0)
                            em.op("g", lambda e, s=s, t0=t0, spar=spar: e.dma_start(
                                out=xblk[spar][:],
                                in_=x_h[s].rearrange("(t p) d -> p t d", p=128)[:, t0:t0 + TBT, :]),
                                inc=["gxa", "gxb"][spar], incn=16)
                            x_ld2 = em.val(["gxa", "gxb"][spar])
                            em.wait("q", "gsb", em.val("gsb"))
                            em.wait("q", "v", sblk_guard[spar])
                            em.op("q", lambda e, s=s, t0=t0, spar=spar: e.dma_start(
                                out=sblk[spar][:],
                                in_=s_h[s].rearrange("(t p) d -> p t d", p=128)[:, t0:t0 + TBT, :]),
                                inc=["qs0", "qs1"][spar], incn=16)
                            s_ld = em.val(["qs0", "qs1"][spar])
                            em.wait("v", ["gxa", "gxb"][spar], x_ld2)
                            em.wait("v", "t", xnblk_guard[spar])
                            for tt in range(TBT):
                                em.drain("v")
                                em.op("v", lambda e, tt=tt, t0=t0, spar=spar: e.tensor_scalar(
                                    xnblk[spar][:, tt, :], xblk[spar][:, tt, :],
                                    stat_nm[:, t0 + tt:t0 + tt + 1],
                                    stat_is[:, t0 + tt:t0 + tt + 1], ALU.add, ALU.mult), inc="v")
                                em.drain("v")
                                em.op("v", lambda e, tt=tt, t0=t0, spar=spar: e.tensor_scalar(
                                    xnblk[spar][:, tt, :], xnblk[spar][:, tt, :],
                                    stat_a1[:, t0 + tt:t0 + tt + 1], None, ALU.mult), inc="v")
                            xn_info[blk] = (em.val("v"), s_ld)

                        emit_loads_xn(0)
                        for blk in range(NB):
                            t0 = blk * TBT
                            spar = blk % 2
                            xnorm_v, s_ld = xn_info.pop(blk)
                            em.wait("t", "v", xnorm_v)
                            em.wait("t", "s", prev["gelu"])
                            pxt_guard = {}
                            for dc in range(DC):
                                for grp in range(NGRP):
                                    bank = (dc * NGRP + grp) % 2
                                    em.wait("t", "v", pxt_guard.get(bank, 0))
                                    for k in range(GP):
                                        tt = grp * GP + k
                                        em.op("t", lambda e, dc=dc, tt=tt, k=k, bank=bank,
                                              spar=spar: e.matmul(
                                                  p_h1[bank][:, k * 128:(k + 1) * 128],
                                                  xnblk[spar][:, tt, dc * 128:(dc + 1) * 128],
                                                  ident[:, 0:128], start=True, stop=True),
                                              inc="t" if k == GP - 1 else None)
                                    em.wait("v", "t", em.val("t"))
                                    em.drain("v")
                                    pxt_guard[bank] = em.op(
                                        "v", lambda e, dc=dc, grp=grp, bank=bank: e.tensor_copy(
                                            xnormT[:, dc, grp * DBW:(grp + 1) * DBW],
                                            p_h1[bank][:, 0:DBW]), inc="v")
                            xnT_v = em.val("v")
                            xnblk_guard[spar] = em.val("t")
                            if pend_fin[0] is not None:
                                pend_fin[0]()
                                pend_fin[0] = None
                            em.wait("t", "v", xnT_v)
                            if blk == 0:
                                em.wait("t", qu, wupT_ld)
                            gelu = [0] * FT
                            for ft in range(FT):
                                par = ft % 2
                                if ft >= 2:
                                    em.wait("t", "s", gelu[ft - 2])
                                for dc in range(DC):
                                    em.op("t", lambda e, ft=ft, dc=dc, par=par: e.matmul(
                                        p_h1[par][:, :], wupT[:, dc, ft * 128:(ft + 1) * 128],
                                        xnormT[:, dc, :], start=(dc == 0), stop=(dc == DC - 1)),
                                        inc="t" if dc == DC - 1 else None)
                                em.wait("s", "t", em.val("t"))
                                if ft == 0:
                                    em.wait("s", "t", prev["dn"])
                                em.drain("s")
                                gelu[ft] = em.op("s", lambda e, ft=ft, par=par: e.activation(
                                    h1gT[:, ft, :], p_h1[par][:, :], AF.Gelu), inc="s")
                                em.wait("v", "s", gelu[ft])
                                if ft == 0:
                                    em.wait("v", "t", prev["ssqmm"])
                                em.drain("v")
                                em.op("v", lambda e, ft=ft, par=par: e.tensor_tensor(
                                    hsq[par][:], h1gT[:, ft, :], h1gT[:, ft, :], ALU.mult),
                                    inc="v")
                                em.drain("v")
                                if ft == 0:
                                    em.op("v", lambda e: e.tensor_copy(hacc[:], hsq[0][:]),
                                          inc="v")
                                else:
                                    em.op("v", lambda e, par=par: e.tensor_tensor(
                                        hacc[:], hacc[:], hsq[par][:], ALU.add), inc="v")
                                em.drain("v")
                            gelu_all = gelu[FT - 1]
                            prev["gelu"] = gelu_all
                            em.op("v", lambda e: e.tensor_copy(hsq[0][:], hacc[:]), inc="v")
                            hacc_v = em.val("v")
                            em.wait("t", "v", hacc_v)
                            em.wait("t", "v", prev["sq"])
                            em.op("t", lambda e: e.matmul(
                                p_ssq[0:1, :], ones_cb[:, 0:1], hsq[0][:],
                                start=True, stop=True), inc="t")
                            prev["ssqmm"] = ssq_t = em.val("t")
                            sil_s = prev["sil"]

                            def emit_down_mms(db, sil_guard):
                                for fc in range(FT):
                                    wpar = fc % 6
                                    if s + blk + db + fc == 0:
                                        em.wait("q", "gwa", tern_gw[0])
                                        em.wait("q", "gwb", tern_gw[1])
                                    em.wait("q", "t", wdn_guard[wpar])
                                    em.op("q", lambda e, fc=fc, db=db, wpar=wpar:
                                          e.dma_start_transpose(
                                              out=wdnc[wpar][:],
                                              in_=wdnt_h[db * DBW:(db + 1) * DBW,
                                                         fc * 128:(fc + 1) * 128]),
                                          inc=f"qd{wpar}", incn=16)
                                    em.wait("t", f"qd{wpar}", em.val(f"qd{wpar}"))
                                    if fc == 0:
                                        em.wait("t", "s", gelu_all)
                                        em.wait("t", "s", sil_guard)
                                    for tt in range(TBT):
                                        em.op("t", lambda e, fc=fc, tt=tt, wpar=wpar: e.matmul(
                                            p_dn[tt][:, :], h1gT[:, fc, tt * 128:(tt + 1) * 128],
                                            wdnc[wpar][:, :], start=(fc == 0), stop=(fc == FT - 1)),
                                            inc="t" if tt == TBT - 1 else None)
                                    wdn_guard[wpar] = em.val("t")
                                return em.val("t")

                            dn_t0 = emit_down_mms(0, sil_s)
                            em.wait("v", "t", ssq_t)
                            em.wait("v", "t", prev["ssc"])
                            em.drain("v")
                            prev["sq"] = em.op(
                                "v", lambda e: e.tensor_copy(ssqr[0:1, :], p_ssq[0:1, :]), inc="v")
                            em.wait("t", "v", prev["sq"])
                            em.wait("t", "s", prev["sq2"])
                            for tt in range(TBT):
                                em.op("t", lambda e, tt=tt: e.matmul(
                                    p_ssc[:, tt:tt + 1], ssqr[0:1, tt * 128:(tt + 1) * 128],
                                    one11[0:1, 0:1], start=True, stop=True),
                                    inc="t" if tt == TBT - 1 else None)
                            prev["ssc"] = em.val("t")
                            em.wait("s", "t", prev["ssc"])
                            em.drain("s")
                            prev["sq2"] = em.op(
                                "s", lambda e: e.activation(sqs[:, :], p_ssc[:, :], AF.Sqrt), inc="s")
                            em.wait("v", "s", prev["sq2"])
                            em.wait("v", "s", prev["sil"])
                            em.drain("v")
                            em.op("v", lambda e: e.tensor_scalar(
                                a2tmp[:, :], sqs[:, :], 1.0 / float(np.sqrt(DFF)), 1e-8,
                                ALU.mult, ALU.add), inc="v")
                            em.drain("v")
                            em.op("v", lambda e: e.reciprocal(alpha2[:, :], a2tmp[:, :]), inc="v")
                            a2_v = em.val("v")
                            if blk + 1 < NB:
                                emit_loads_xn(blk + 1)
                            em.wait("s", "t", dn_t0)
                            em.wait("s", "v", a2_v)
                            em.wait("s", "v", prev["fin"])
                            for tt in range(TBT):
                                em.drain("s")
                                em.op("s", lambda e, tt=tt: e.activation(
                                    gblk[:, tt, 0:DBW], p_dn[tt][:, :],
                                    AF.Silu, scale=alpha2[:, tt:tt + 1]), inc="s")
                            sil_s = em.val("s")
                            dn_t1 = emit_down_mms(1, sil_s)
                            em.wait("s", "t", dn_t1)
                            for tt in range(TBT):
                                em.drain("s")
                                em.op("s", lambda e, tt=tt: e.activation(
                                    gblk[:, tt, DBW:2 * DBW], p_dn[tt][:, :],
                                    AF.Silu, scale=alpha2[:, tt:tt + 1]), inc="s")
                            sil_s = em.val("s")
                            prev["dn"] = em.val("t")
                            prev["sil"] = sil_s

                            def fin_fn(blk=blk, spar=spar, s_ld=s_ld, sil_s=sil_s, t0=t0):
                                em.wait("v", "s", sil_s)
                                em.wait("v", ["qs0", "qs1"][spar], s_ld)
                                em.wait("v", "goa", em.val("goa"))
                                em.wait("v", "gob", em.val("gob"))
                                for tt in range(TBT):
                                    em.drain("v")
                                    em.op("v", lambda e, tt=tt, spar=spar: e.tensor_tensor(
                                        oblk[spar][:, tt, :], sblk[spar][:, tt, :], gblk[:, tt, :],
                                        ALU.mult), inc="v")
                                    em.drain("v")
                                    em.op("v", lambda e, tt=tt, spar=spar: e.tensor_tensor(
                                        oblk[spar][:, tt, :], oblk[spar][:, tt, :],
                                        xblk[spar][:, tt, :], ALU.add), inc="v")
                                fin_v = em.val("v")
                                sblk_guard[spar] = fin_v
                                xblk_guard[spar] = fin_v
                                prev["fin"] = fin_v
                                em.wait("g", "v", fin_v)
                                em.op("g", lambda e, s=s, t0=t0, spar=spar: e.dma_start(
                                    out=out_h[s].rearrange("(t p) d -> p t d", p=128)[
                                        :, t0:t0 + TBT, :],
                                    in_=oblk[spar][:]), inc=["goa", "gob"][spar], incn=16)
                            pend_fin[0] = fin_fn
                        if pend_fin[0] is not None:
                            pend_fin[0]()
                            pend_fin[0] = None

            _do_seq()
        em.wait("g", "goa", em.val("goa"))
        em.wait("g", "gob", em.val("gob"))

        with nc.Block() as block:
            @block.tensor
            def _(eng):
                for f in em.ops["t"]:
                    f(eng)

            @block.scalar
            def _(eng):
                for f in em.ops["s"]:
                    f(eng)

            @block.vector
            def _(eng):
                for f in em.ops["v"]:
                    f(eng)

            @block.sync
            def _(eng):
                for f in em.ops["q"]:
                    f(eng)

            @block.gpsimd
            def _(eng):
                for f in em.ops["g"]:
                    f(eng)

    return nc


def kernel(L0_attn_x, L0_attn_w_qkv, L0_attn_w_out, L0_ffn_w_up, L0_ffn_w_down):
    x = np.ascontiguousarray(L0_attn_x, dtype=np.float32)
    wqkv = np.ascontiguousarray(L0_attn_w_qkv, dtype=np.float32)
    wout = np.ascontiguousarray(L0_attn_w_out, dtype=np.float32)
    wup = np.ascontiguousarray(L0_ffn_w_up, dtype=np.float32)
    wdn = np.ascontiguousarray(L0_ffn_w_down, dtype=np.float32)
    ident = np.eye(128, dtype=ml_dtypes.bfloat16)
    n = 8
    nc = build(FULL)
    in_maps = [{
        "x": x[2 * c:2 * c + 2],
        "w_qkv": wqkv, "w_out": wout, "w_up": wup, "w_down": wdn,
        "ident": ident,
    } for c in range(n)]
    res = run_bass_kernel_spmd(nc, in_maps, core_ids=list(range(n)))
    return np.concatenate([r["out"] for r in res.results], axis=0)


# version 25
# speedup vs baseline: 1.0678x; 1.0037x over previous
import numpy as np
import ml_dtypes
from contextlib import ExitStack

import concourse.bass as bass
import concourse.mybir as mybir
from concourse.bass_utils import run_bass_kernel_spmd

F32 = mybir.dt.float32
BF16 = mybir.dt.bfloat16
ALU = mybir.AluOpType
AF = mybir.ActivationFunctionType
AX = mybir.AxisListType.X

FULL = dict(B=2, S=2048, D=1024, H=16, DFF=4096)


class Em:
    ENGINES = ("t", "s", "v", "q", "g")

    def __init__(self):
        self.ops = {e: [] for e in self.ENGINES}
        self.cnt = {}
        self.sems = {}

    def op(self, e, fn, inc=None, incn=1):
        v = None
        if inc is not None:
            v = self.cnt.get(inc, 0) + incn
            self.cnt[inc] = v

        def run(eng, fn=fn, inc=inc, incn=incn):
            r = fn(eng)
            if inc is not None:
                r.then_inc(self.sems[inc], incn)

        self.ops[e].append(run)
        return v

    def wait(self, e, sem, v):
        assert v <= self.cnt.get(sem, 0), f"wait {sem}>={v}, only {self.cnt.get(sem, 0)} planned"
        if v <= 0:
            return
        self.ops[e].append(lambda eng, sem=sem, v=v: eng.wait_ge(self.sems[sem], v))

    def drain(self, e):
        self.ops[e].append(lambda eng: eng.drain())

    def val(self, sem):
        return self.cnt.get(sem, 0)


def build(cfg):
    B, S, D, H, DFF = cfg["B"], cfg["S"], cfg["D"], cfg["H"], cfg["DFF"]
    DH, WIN, DIL = 64, 128, 4
    DC = D // 128
    HP = H // 2
    NT = S // 128
    ND = S // DIL
    NJD = ND // 128
    NW = S // WIN
    SH = S // 2
    NTH = NT // 2
    NWH = NW // 2
    NIS_D = (ND // 2) // 128
    TB = 512
    NB = S // TB
    TBT = TB // 128
    FT = DFF // 128
    DBW = min(512, D)
    NDB = D // DBW
    SB512 = S // 512
    assert H * DH == D and NWH % 4 == 0 and SH % 512 == 0 and ND % 256 == 0

    nc = bass.Bass("TRN2", target_bir_lowering=False)
    x_h = nc.dram_tensor("x", [B, S, D], F32, kind="ExternalInput").ap()
    wqkv_h = nc.dram_tensor("w_qkv", [D, 3 * D], F32, kind="ExternalInput").ap()
    wout_h = nc.dram_tensor("w_out", [D, D], F32, kind="ExternalInput").ap()
    wup_h = nc.dram_tensor("w_up", [DFF, D], F32, kind="ExternalInput").ap()
    wdn_h = nc.dram_tensor("w_down", [D, DFF], F32, kind="ExternalInput").ap()
    ident_h = nc.dram_tensor("ident", [128, 128], BF16, kind="ExternalInput").ap()
    out_h = nc.dram_tensor("out", [B, S, D], F32, kind="ExternalOutput").ap()
    xn_h = nc.dram_tensor("xn_scr", [B, S, D], BF16, kind="Internal").ap()
    s_h = nc.dram_tensor("s_scr", [B, S, D], BF16, kind="Internal").ap()
    wupt_h = nc.dram_tensor("wupt_scr", [DFF, D], BF16, kind="Internal").ap()
    wdnt_h = nc.dram_tensor("wdnt_scr", [D, DFF], BF16, kind="Internal").ap()

    em = Em()
    st = ExitStack()
    with st:
        sb = lambda name, shape, dt: st.enter_context(nc.sbuf_tensor(name, shape, dt))
        for sn in ("t", "s", "v", "qa", "qw0", "qw1", "qx0", "qx1", "qt0", "qt1",
                   "qu0", "qu1", "qd0", "qd1", "qd2", "qd3", "qd4", "qd5", "qs0", "qs1",
                   "gwa", "gwb", "gna", "gnb", "gsb", "gxa", "gxb", "goa", "gob"):
            em.sems[sn] = st.enter_context(nc.semaphore(sn))

        ident = sb("ident_sb", [128, 128], BF16)
        ones_cb = sb("ones_cb", [128, 1], BF16)
        one11 = sb("one11", [1, 1], F32)
        ones_cf = sb("ones_cf", [128, 1], F32)
        ones_rf = sb("ones_rf", [1, 128], F32)
        thrp = sb("thrp", [128, 1], F32)
        thrn = sb("thrn", [128, 1], F32)
        thr11 = sb("thr11", [1, 1], F32)
        stat_nm = sb("stat_nm", [128, NT], F32)
        stat_is = sb("stat_is", [128, NT], F32)
        stat_a1 = sb("stat_a1", [128, NT], F32)
        scr_a = sb("scr_a", [128, NT], F32)
        scr_b = sb("scr_b", [128, NT], F32)
        red1 = sb("red1", [128, 1], F32)
        trash = sb("trash", [128, D], BF16)

        em.op("q", lambda e: e.dma_start(out=ident[:], in_=ident_h[:]), inc="qa", incn=16)
        ld_ident = em.val("qa")
        em.op("v", lambda e: e.memset(ones_cb[:], 1.0), inc="v")
        em.op("v", lambda e: e.memset(one11[0:1, :], 1.0), inc="v")
        em.op("v", lambda e: e.memset(ones_cf[:], 1.0), inc="v")
        em.op("v", lambda e: e.memset(ones_rf[0:1, :], 1.0), inc="v")

        CMAX = max(D, DFF)
        with ExitStack() as p2:
            sb2 = lambda name, shape, dt: p2.enter_context(nc.sbuf_tensor(name, shape, dt))
            NCUP = 26
            wch = [sb2(f"wch{i}", [128, CMAX], F32) for i in range(2)]
            wcache = sb2("wcache", [128, NCUP * D], F32)
            btrash = sb2("btrash", [128, CMAX], BF16)
            wtc = [sb2(f"wtc{i}", [128, CMAX], BF16) for i in range(2)]
            ncmp = sb2("ncmp", [128, CMAX], F32)
            asums = sb2("asums", [128, 32], F32)
            p_red = p2.enter_context(nc.psum_tensor("p_red", [128, 2], F32))

            wch_guard = [("s", 0), ("s", 0)]
            wtc_guard = [0, 0]
            qsem = ["qw0", "qw1"]
            for wi, (w_src, wt_dst, R, C) in enumerate(
                    [(wup_h, wupt_h, DFF, D), (wdn_h, wdnt_h, D, DFF)]):
                NCH = R // 128
                for c in range(NCH):
                    par = c % 2
                    cached = (wi == 0 and c < NCUP)
                    if cached:
                        buf = lambda c=c: wcache[:, c * D:(c + 1) * D]
                    else:
                        buf = lambda par=par, C=C: wch[par][:, 0:C]
                        geng, gval = wch_guard[par]
                        em.wait("q", geng, gval)
                    em.op("q", lambda e, c=c, buf=buf, w_src=w_src, C=C: e.dma_start(
                        out=buf(), in_=w_src[c * 128:(c + 1) * 128, :]),
                        inc=qsem[par], incn=16)
                    em.wait("s", qsem[par], em.val(qsem[par]))
                    em.drain("s")
                    scnt = em.op(
                        "s", lambda e, c=c, buf=buf, C=C: e.activation(
                            btrash[:, 0:C], buf(), AF.Abs,
                            accum_out=asums[:, c:c + 1]), inc="s")
                    if not cached:
                        wch_guard[par] = ("s", scnt)
                em.wait("v", "s", em.val("s"))
                em.drain("v")
                em.op("v", lambda e, NCH=NCH: e.reduce_sum(red1[:, 0:1], asums[:, 0:NCH], AX),
                      inc="v")
                em.wait("t", "v", em.val("v"))
                em.wait("t", "qa", ld_ident)
                em.op("t", lambda e: e.matmul(p_red[0:1, 0:1], ones_cf[:, 0:1], red1[:, 0:1],
                                              start=True, stop=True), inc="t")
                em.wait("v", "t", em.val("t"))
                em.op("v", lambda e, R=R, C=C: e.tensor_scalar(
                    thr11[0:1, 0:1], p_red[0:1, 0:1], 0.5 / (R * C), None, ALU.mult), inc="v")
                em.wait("t", "v", em.val("v"))
                em.op("t", lambda e: e.matmul(p_red[:, 1:2], ones_rf[0:1, :], thr11[0:1, 0:1],
                                              start=True, stop=True), inc="t")
                em.wait("v", "t", em.val("t"))
                em.op("v", lambda e: e.tensor_copy(thrp[:, 0:1], p_red[:, 1:2]), inc="v")
                em.drain("v")
                em.op("v", lambda e: e.tensor_scalar(thrn[:, 0:1], thrp[:, 0:1], -1.0, None,
                                                     ALU.mult), inc="v")
                em.drain("v")
                for c in range(NCH):
                    par = c % 2
                    cached = (wi == 0 and c < NCUP)
                    if cached:
                        buf = lambda c=c: wcache[:, c * D:(c + 1) * D]
                    else:
                        buf = lambda par=par, C=C: wch[par][:, 0:C]
                        geng, gval = wch_guard[par]
                        em.wait("q", geng, gval)
                        em.op("q", lambda e, c=c, par=par, w_src=w_src, C=C: e.dma_start(
                            out=wch[par][:, 0:C], in_=w_src[c * 128:(c + 1) * 128, :]),
                            inc=qsem[par], incn=16)
                        em.wait("v", qsem[par], em.val(qsem[par]))
                    em.drain("v")
                    em.op("v", lambda e, buf=buf, C=C: e.tensor_single_scalar(
                        ncmp[:, 0:C], buf(), thrn[:, 0:1], ALU.is_lt), inc="v")
                    em.drain("v")
                    em.wait("v", ["gwa", "gwb"][par], wtc_guard[par])
                    vcnt = em.op(
                        "v", lambda e, par=par, buf=buf, C=C: e.scalar_tensor_tensor(
                            wtc[par][:, 0:C], buf(), thrp[:, 0:1], ncmp[:, 0:C],
                            ALU.is_gt, ALU.subtract), inc="v")
                    if not cached:
                        wch_guard[par] = ("v", vcnt)
                    em.wait("g", "v", vcnt)
                    wtc_guard[par] = em.op(
                        "g", lambda e, c=c, par=par, wt_dst=wt_dst, C=C: e.dma_start(
                            out=wt_dst[c * 128:(c + 1) * 128, :], in_=wtc[par][:, 0:C]),
                        inc=["gwa", "gwb"][par], incn=16)
            tern_gw = (em.val("gwa"), em.val("gwb"))

        for s in range(B):
            def _do_seq(s=s):
                qx, qt, qu = f"qx{s}", f"qt{s}", f"qu{s}"
                with ExitStack() as sq:
                    sbq = lambda name, shape, dt: sq.enter_context(nc.sbuf_tensor(name, shape, dt))
                    with ExitStack() as p3:
                        xs = p3.enter_context(nc.sbuf_tensor(f"xs{s}", [128, NT, D], F32))
                        xnb = [p3.enter_context(nc.sbuf_tensor(f"xnb{s}{i}", [128, D], BF16))
                               for i in range(2)]
                        for gsem_ in ("v", "s", "t", "gwa", "gwb", "gna", "gnb", "gsb", "goa", "gob"):
                            em.wait("q", gsem_, em.val(gsem_))
                        for i in range(4):
                            em.op("q", lambda e, s=s, i=i: e.dma_start(
                                out=xs[:, i * (NT // 4):(i + 1) * (NT // 4), :],
                                in_=x_h[s].rearrange("(t p) d -> p t d", p=128)[
                                    :, i * (NT // 4):(i + 1) * (NT // 4), :]), inc=qx, incn=16)
                        x_ld = em.val(qx)
                        em.wait("s", qx, x_ld)
                        em.wait("v", qx, x_ld)
                        for tt in range(NT):
                            em.drain("s")
                            em.op("s", lambda e, tt=tt: e.activation(
                                trash[:, 0:D], xs[:, tt, :], AF.Square,
                                accum_out=scr_a[:, tt:tt + 1]), inc="s")
                            em.op("v", lambda e, tt=tt: e.reduce_sum(
                                scr_b[:, tt:tt + 1], xs[:, tt, :], AX), inc="v")
                            em.drain("v")
                        em.drain("v")
                        em.op("v", lambda e: e.tensor_scalar(stat_nm[:], scr_b[:], -1.0 / D, None,
                                                             ALU.mult), inc="v")
                        em.wait("v", "s", em.val("s"))
                        em.drain("v")
                        em.op("v", lambda e: e.tensor_scalar(scr_a[:], scr_a[:], 1.0 / D, None,
                                                             ALU.mult), inc="v")
                        em.drain("v")
                        em.op("v", lambda e: e.tensor_tensor(scr_b[:], stat_nm[:], stat_nm[:],
                                                             ALU.mult), inc="v")
                        em.drain("v")
                        em.op("v", lambda e: e.tensor_tensor(scr_a[:], scr_a[:], scr_b[:],
                                                             ALU.subtract), inc="v")
                        em.drain("v")
                        em.op("v", lambda e: e.tensor_scalar(scr_b[:], scr_a[:], 1e-5, None,
                                                             ALU.add), inc="v")
                        em.drain("v")
                        em.op("v", lambda e: e.reciprocal(scr_b[:], scr_b[:]), inc="v")
                        em.drain("v")
                        em.op("v", lambda e: e.tensor_tensor(scr_a[:], scr_a[:], scr_b[:],
                                                             ALU.mult), inc="v")
                        em.drain("v")
                        vst = em.val("v")
                        em.wait("s", "v", vst)
                        em.drain("s")
                        em.op("s", lambda e: e.activation(stat_is[:], scr_b[:], AF.Sqrt), inc="s")
                        em.drain("s")
                        em.op("s", lambda e: e.activation(scr_a[:], scr_a[:], AF.Sqrt), inc="s")
                        em.wait("v", "s", em.val("s"))
                        em.drain("v")
                        em.op("v", lambda e: e.tensor_scalar(scr_a[:], scr_a[:], 1e-8, None,
                                                             ALU.add), inc="v")
                        em.drain("v")
                        em.op("v", lambda e: e.reciprocal(stat_a1[:], scr_a[:]), inc="v")
                        em.drain("v")
                        em.wait("v", "gwa", tern_gw[0])
                        em.wait("v", "gwb", tern_gw[1])
                        em.wait("v", "go", em.val("go"))
                        em.wait("v", "t", em.val("t"))
                        for tt in range(NT):
                            par = tt % 2
                            gsem = ["gna", "gnb"][par]
                            if tt >= 2:
                                em.wait("v", gsem, em.val(gsem))
                            em.drain("v")
                            em.op("v", lambda e, tt=tt, par=par: e.tensor_scalar(
                                xnb[par][:], xs[:, tt, :], stat_nm[:, tt:tt + 1],
                                stat_is[:, tt:tt + 1], ALU.add, ALU.mult), inc="v")
                            em.wait("g", "v", em.val("v"))
                            em.op("g", lambda e, s=s, tt=tt, par=par, gsem=gsem: e.dma_start(
                                out=xn_h[s].rearrange("(t p) d -> p t d", p=128)[:, tt, :],
                                in_=xnb[par][:]), inc=gsem, incn=16)
                        xn_done = (em.val("gna"), em.val("gnb"))

                    pa = ExitStack()
                    sba = lambda name, shape, dt: pa.enter_context(nc.sbuf_tensor(name, shape, dt))
                    qT = sba(f"qT{s}", [128, HP, S], BF16)
                    kT = sba(f"kT{s}", [128, HP, S], BF16)
                    v_c = sba(f"v_c{s}", [128, NT, H * 65], BF16)
                    v_d = sba(f"v_d{s}", [128, NT, H * 65], BF16)
                    for gsem_ in ("t", "s", "gna", "gnb", "gwa", "gwb", "gsb", "go"):
                        em.wait("v", gsem_, em.val(gsem_))
                    em.op("v", lambda e: e.memset(v_c[:], 3.0), inc="v")
                    em.op("v", lambda e: e.memset(v_d[:], 3.0), inc="v")
                    vone_v = em.val("v")
                    with ExitStack() as p4:
                        xnT = p4.enter_context(nc.sbuf_tensor(f"xnT{s}", [128, DC, S], BF16))
                        wsl = [p4.enter_context(nc.sbuf_tensor(f"wsl{s}{i}", [128, D], F32))
                               for i in range(2)]
                        wqs = [p4.enter_context(nc.sbuf_tensor(f"wqs{s}{i}", [128, DC, D], BF16))
                               for i in range(2)]
                        p_qkv = [p4.enter_context(nc.psum_tensor(f"pqkv{s}{i}", [128, 512], F32))
                                 for i in range(2)]
                        em.wait("q", "gna", xn_done[0])
                        em.wait("q", "gnb", xn_done[1])
                        for dc in range(DC):
                            em.op("q", lambda e, s=s, dc=dc: e.dma_start_transpose(
                                out=xnT[:, dc, :], in_=xn_h[s][:, dc * 128:(dc + 1) * 128]),
                                inc=qt, incn=16)
                        xnT_ld = em.val(qt)

                        cast_v = [0, 0]
                        evac_eng = [0]

                        def qkv_evac(out_f, in_f):
                            eng = "v" if evac_eng[0] % 2 == 0 else "s"
                            evac_eng[0] += 1
                            em.wait(eng, "t", em.val("t"))
                            em.drain(eng)
                            if eng == "v":
                                cnt = em.op("v", lambda e, o=out_f, i=in_f: e.tensor_copy(o(), i()),
                                            inc="v")
                            else:
                                cnt = em.op("s", lambda e, o=out_f, i=in_f: e.copy(o(), i()),
                                            inc="s")
                            return (eng, cnt)

                        def load_wslice(foff, slot, t_guard):
                            for dc in range(DC):
                                par = dc % 2
                                em.wait("q", "t", t_guard)
                                em.wait("q", "v", cast_v[par])
                                em.op("q", lambda e, dc=dc, par=par, foff=foff: e.dma_start(
                                    out=wsl[par][:], in_=wqkv_h[dc * 128:(dc + 1) * 128,
                                                               foff:foff + D]),
                                    inc=qsem[par], incn=16)
                                em.wait("v", qsem[par], em.val(qsem[par]))
                                cast_v[par] = em.op(
                                    "v", lambda e, dc=dc, par=par, slot=slot: e.tensor_copy(
                                        wqs[slot][:, dc, :], wsl[par][:]), inc="v")
                            return em.val("v")

                        qsem = ["qw0", "qw1"]
                        pq_free = [("v", 0), ("v", 0)]
                        em.wait("t", qt, xnT_ld)
                        mmc = 0
                        cast_done = [0, 0]
                        cast_done[0] = load_wslice(0, 0, em.val("t"))
                        cast_done[1] = load_wslice(D, 1, em.val("t"))
                        for qk in range(2):
                            em.wait("t", "v", cast_done[qk])
                            dst = qT if qk == 0 else kT
                            for c in range(HP):
                                for tb in range(SB512):
                                    par = mmc % 2
                                    mmc += 1
                                    em.wait("t", *pq_free[par])
                                    for dc in range(DC):
                                        em.op("t", lambda e, c=c, tb=tb, dc=dc, par=par, qk=qk:
                                              e.matmul(
                                                  p_qkv[par][:, :],
                                                  wqs[qk][:, dc, c * 128:(c + 1) * 128],
                                                  xnT[:, dc, tb * 512:(tb + 1) * 512],
                                                  start=(dc == 0), stop=(dc == DC - 1)),
                                              inc="t" if dc == DC - 1 else None)
                                    pq_free[par] = qkv_evac(
                                        lambda c=c, tb=tb, dst=dst: dst[:, c, tb * 512:(tb + 1) * 512],
                                        lambda par=par: p_qkv[par][:, :])
                            if qk == 0:
                                q_mms_done = em.val("t")
                        vcast = load_wslice(2 * D, 0, q_mms_done)
                        em.wait("t", "v", vcast)
                        em.wait("t", "v", vone_v)
                        for which, dstv in ((0, v_c), (1, v_d)):
                            for tt in range(NT):
                                for fb in range(NDB):
                                    par = mmc % 2
                                    mmc += 1
                                    em.wait("t", *pq_free[par])
                                    for dc in range(DC):
                                        if which == 0:
                                            lhs_ap = (lambda dc=dc, tt=tt:
                                                      xnT[:, dc, tt * 128:(tt + 1) * 128])
                                        else:
                                            g0_, j0_ = (tt * 128) // ND, (tt * 128) % ND
                                            lhs_ap = (lambda dc=dc, g0_=g0_, j0_=j0_: xnT[
                                                :, dc, g0_ + DIL * j0_: g0_ + DIL * (j0_ + 127) + 1: DIL])
                                        em.op("t", lambda e, lhs_ap=lhs_ap, fb=fb, dc=dc, par=par:
                                              e.matmul(p_qkv[par][:, 0:DBW], lhs_ap(),
                                                       wqs[0][:, dc, fb * DBW:(fb + 1) * DBW],
                                                       start=(dc == 0), stop=(dc == DC - 1)),
                                              inc="t" if dc == DC - 1 else None)
                                    nh = DBW // 64
                                    pq_free[par] = qkv_evac(
                                        lambda tt=tt, fb=fb, dstv=dstv, nh=nh:
                                        dstv[:, tt].rearrange("p (h c) -> p h c", c=65)[
                                            :, fb * nh:(fb + 1) * nh, 0:64],
                                        lambda par=par: p_qkv[par][:, 0:DBW].rearrange(
                                            "p (h c) -> p h c", c=64))
                    qkv_t = em.val("t")
                    qkv_done = (em.val("v"), em.val("s"))

                    s_half = sba(f"s_half{s}", [128, NTH, D], BF16)
                    woutb = sba(f"woutb{s}", [128, DC, D], BF16)
                    wslk = [sba(f"wslk{s}{i}", [128, D], F32) for i in range(2)]
                    wob_v = [0, 0]
                    em.wait("q", "t", qkv_t)
                    em.wait("q", "v", em.val("v"))
                    for dc in range(DC):
                        par = dc % 2
                        em.wait("q", "v", wob_v[par])
                        em.op("q", lambda e, dc=dc, par=par: e.dma_start(
                            out=wslk[par][:], in_=wout_h[dc * 128:(dc + 1) * 128, :]),
                            inc=["qw0", "qw1"][par], incn=16)
                        em.wait("v", ["qw0", "qw1"][par], em.val(["qw0", "qw1"][par]))
                        wob_v[par] = em.op("v", lambda e, dc=dc, par=par: e.tensor_copy(
                            woutb[:, dc, :], wslk[par][:]), inc="v")
                    wout_cast = em.val("v")

                    for half in range(2):
                        def _do_half(half=half):
                            with ExitStack() as ph:
                                onormT = ph.enter_context(
                                    nc.sbuf_tensor(f"onT{s}{half}", [128, DC, SH], BF16))
                                with ExitStack() as p5:
                                    etb = [p5.enter_context(
                                        nc.sbuf_tensor(f"et{s}{half}{i}", [128, 512], BF16))
                                        for i in range(4)]
                                    otr = [p5.enter_context(
                                        nc.sbuf_tensor(f"otr{s}{half}{i}", [65, 512], BF16))
                                        for i in range(4)]
                                    onrm = [p5.enter_context(
                                        nc.sbuf_tensor(f"onrm{s}{half}{i}", [128, 256], BF16))
                                        for i in range(2)]
                                    rec = [p5.enter_context(
                                        nc.sbuf_tensor(f"rec{s}{half}{i}", [128, 4], F32))
                                        for i in range(2)]
                                    p_sc = [p5.enter_context(
                                        nc.psum_tensor(f"psc{s}{half}{h}{i}", [128, 512], F32))
                                        for h in range(2) for i in range(2)]
                                    p_oT = [p5.enter_context(
                                        nc.psum_tensor(f"poT{s}{half}{i}", [65, 512], F32))
                                        for i in range(2)]
                                    p_tr = p5.enter_context(
                                        nc.psum_tensor(f"ptr{s}{half}", [128, 260], F32))
                                    p_bk = p5.enter_context(
                                        nc.psum_tensor(f"pbk{s}{half}", [128, 512], F32))

                                    gd = dict(psc=[[0, 0], [0, 0]], evac=[0, 0],
                                              otr_g=[[0, 0], [0, 0]],
                                              onrm=[0, 0], ptr=0, pbk=0, av=[0, 0])
                                    pend = [None, None]
                                    ucnt = [0]

                                    def emit_evacs(NIS, upar):
                                        NIC = NIS * 128
                                        ev = [0, 0]
                                        for hi in range(2):
                                            em.wait("s", "t", gd["av"][hi])
                                            em.wait("s", "t", gd["otr_g"][hi][upar])
                                            em.drain("s")
                                            ev[hi] = em.op(
                                                "s", lambda e, hi=hi, NIC=NIC, upar=upar: e.copy(
                                                    otr[hi * 2 + upar][0:65, 0:NIC],
                                                    p_oT[hi][0:65, 0:NIC]), inc="s")
                                        gd["evac"] = ev[:]
                                        return ev

                                    def tail_fwd_one(NIS, upar, ev, hi, prev_ts):
                                        em.wait("t", "s", ev[hi])
                                        em.wait("t", "v", gd["ptr"] if hi == 0 else prev_ts)
                                        for k in range(NIS):
                                            em.op("t", lambda e, hi=hi, k=k, upar=upar: e.matmul(
                                                p_tr[:, k * 65:(k + 1) * 65],
                                                otr[hi * 2 + upar][0:65, k * 128:(k + 1) * 128],
                                                ident[0:65, 0:65], start=True, stop=True),
                                                inc="t" if k == NIS - 1 else None)
                                        gd["otr_g"][hi][upar] = em.val("t")
                                        em.wait("v", "t", em.val("t"))
                                        em.wait("v", "t", gd["onrm"][hi])
                                        em.drain("v")
                                        em.op("v", lambda e, hi=hi, NIS=NIS: e.reciprocal(
                                            rec[hi][:, 0:NIS], p_tr[:, 64:NIS * 65:65]), inc="v")
                                        em.drain("v")
                                        for k in range(NIS):
                                            em.op("v", lambda e, hi=hi, k=k: e.tensor_scalar(
                                                onrm[hi][:, k * 64:(k + 1) * 64],
                                                p_tr[:, k * 65:k * 65 + 64],
                                                rec[hi][:, k:k + 1], None, ALU.mult), inc="v")
                                            em.drain("v")
                                        return em.val("v")

                                    def tail_rest(NIS, merge, ts_done):
                                        em.wait("t", "v", ts_done[1])
                                        em.wait("t", "v", gd["pbk"])
                                        for hi in range(2):
                                            for k in range(NIS):
                                                em.op("t", lambda e, hi=hi, k=k: e.matmul(
                                                    p_bk[hi * 64:(hi + 1) * 64, k * 128:(k + 1) * 128],
                                                    onrm[hi][:, k * 64:(k + 1) * 64], ident[:, 0:128],
                                                    start=True, stop=True, tile_position=(0, hi * 64)),
                                                    inc="t" if (hi == 1 and k == NIS - 1) else None)
                                        bk_t = em.val("t")
                                        gd["onrm"] = [bk_t, bk_t]
                                        em.wait("v", "t", bk_t)
                                        em.drain("v")
                                        merge()
                                        gd["pbk"] = em.val("v")

                                    def run_pend_fwd():
                                        if pend[0] is not None:
                                            NIS, merge, upar, ev, st0 = pend[0]
                                            if st0 is None:
                                                ts0 = tail_fwd_one(NIS, upar, ev, 0, 0)
                                                pend[0] = (NIS, merge, upar, ev, ts0)
                                            else:
                                                ts1 = tail_fwd_one(NIS, upar, ev, 1, st0)
                                                gd["ptr"] = ts1
                                                pend[0] = None
                                                pend[1] = (NIS, merge, (st0, ts1))

                                    def run_pend_rest():
                                        run_pend_fwd()
                                        run_pend_fwd()
                                        if pend[1] is not None:
                                            NIS, merge, ts_done = pend[1]
                                            pend[1] = None
                                            tail_rest(NIS, merge, ts_done)

                                    def finish_unit(NIS, merge):
                                        upar = ucnt[0] % 2
                                        ucnt[0] += 1
                                        run_pend_rest()
                                        ev = emit_evacs(NIS, upar)
                                        pend[0] = (NIS, merge, upar, ev, None)

                                    def score_av_loop(hp, jlist, NIS, sc_ap, av_st, locmode=False,
                                                      mid_hook=None):
                                        NIC = NIS * 128
                                        hA, hB = 2 * hp, 2 * hp + 1
                                        NJ = len(jlist)
                                        if locmode:
                                            tsc = [0, 0]
                                            for jn, j in enumerate(jlist):
                                                for hi, h in enumerate((hA, hB)):
                                                    if jn == 0:
                                                        em.wait("t", "s", gd["psc"][hi][0])
                                                    lhsT, rhs, c0 = sc_ap(h, j)
                                                    em.op("t", lambda e, lhsT=lhsT, rhs=rhs, c0=c0, hi=hi:
                                                          e.matmul(p_sc[hi * 2][:, c0:c0 + 128],
                                                                   lhsT(), rhs(),
                                                                   start=True, stop=True), inc="t")
                                                    tsc[hi] = em.val("t")
                                            exps = [0, 0]
                                            for hi in range(2):
                                                em.wait("s", "t", tsc[hi])
                                                em.op("s", lambda e, hi=hi, NIC=NIC: e.activation(
                                                    etb[hi * 2][:, 0:NIC], p_sc[hi * 2][:, 0:NIC],
                                                    AF.Exp, scale=0.125), inc="s")
                                                exps[hi] = em.val("s")
                                                gd["psc"][hi][0] = exps[hi]
                                            if mid_hook is not None:
                                                mid_hook()
                                                mid_hook()
                                            for hi, h in enumerate((hA, hB)):
                                                em.wait("t", "s", exps[hi])
                                                em.wait("t", "s", gd["evac"][hi])
                                                for jn, j in enumerate(jlist):
                                                    em.op("t", lambda e, h=h, hi=hi, j=j, jn=jn: e.matmul(
                                                        p_oT[hi][0:65, jn * 128:(jn + 1) * 128],
                                                        av_st(h, j)(),
                                                        etb[hi * 2][:, jn * 128:(jn + 1) * 128],
                                                        start=True, stop=True),
                                                        inc="t" if jn == NJ - 1 else None)
                                                gd["av"][hi] = em.val("t")
                                            return
                                        tsc = [[0] * NJ for _ in range(2)]
                                        exps = [[0] * NJ for _ in range(2)]

                                        def emit_av(jn):
                                            par = jn % 2
                                            for hi, h in enumerate((hA, hB)):
                                                em.wait("t", "s", exps[hi][jn])
                                                if jn == 0:
                                                    em.wait("t", "s", gd["evac"][hi])
                                                em.op("t", lambda e, h=h, hi=hi, j=jlist[jn], par=par,
                                                      jn=jn, NIC=NIC: e.matmul(
                                                          p_oT[hi][0:65, 0:NIC], av_st(h, j)(),
                                                          etb[hi * 2 + par][:, 0:NIC],
                                                          start=(jn == 0), stop=(jn == NJ - 1)),
                                                      inc="t" if jn == NJ - 1 else None)
                                                if jn == NJ - 1:
                                                    gd["av"][hi] = em.val("t")

                                        for jn, j in enumerate(jlist):
                                            par = jn % 2
                                            for hi, h in enumerate((hA, hB)):
                                                if jn >= 2:
                                                    em.wait("t", "s", exps[hi][jn - 2])
                                                else:
                                                    em.wait("t", "s", gd["psc"][hi][par])
                                                lhsT, rhs, _ = sc_ap(h, j)
                                                em.op("t", lambda e, lhsT=lhsT, rhs=rhs, hi=hi, par=par,
                                                      NIC=NIC: e.matmul(
                                                          p_sc[hi * 2 + par][:, 0:NIC], lhsT(), rhs(),
                                                          start=True, stop=True), inc="t")
                                                tsc[hi][jn] = em.val("t")
                                            for hi in range(2):
                                                em.wait("s", "t", tsc[hi][jn])
                                                em.op("s", lambda e, hi=hi, par=par, NIC=NIC: e.activation(
                                                    etb[hi * 2 + par][:, 0:NIC],
                                                    p_sc[hi * 2 + par][:, 0:NIC],
                                                    AF.Exp, scale=0.125), inc="s")
                                                exps[hi][jn] = em.val("s")
                                                gd["psc"][hi][par] = exps[hi][jn]
                                            if jn >= 1:
                                                emit_av(jn - 1)
                                            if mid_hook is not None and (
                                                    jn == min(1, NJ - 2) or jn == min(3, NJ - 1)):
                                                mid_hook()
                                        emit_av(NJ - 1)

                                    hd = lambda h: ((h % 2) * 64, h // 2)

                                    em.wait("t", "v", em.val("v"))
                                    em.wait("t", "s", em.val("s"))
                                    for hp in range(HP):
                                        for u in range(NWH // 4):
                                            w0 = half * NWH + u * 4
                                            wl = list(range(w0, w0 + 4))

                                            def sc_ap(h, w, w0=w0):
                                                p0, c = hd(h)
                                                return (lambda p0=p0, c=c, w=w:
                                                        kT[p0:p0 + 64, c, w * 128:(w + 1) * 128],
                                                        lambda p0=p0, c=c, w=w:
                                                        qT[p0:p0 + 64, c, w * 128:(w + 1) * 128],
                                                        (w - w0) * 128)

                                            def av_st(h, w):
                                                return lambda h=h, w=w: v_c[:, w, h * 65:h * 65 + 65]

                                            score_av_loop(hp, wl, 4, sc_ap, av_st, locmode=True,
                                                          mid_hook=run_pend_fwd)

                                            def merge(hp=hp, w0=w0, half=half):
                                                c0 = (w0 - half * NWH) * 128
                                                em.op("v", lambda e, hp=hp, c0=c0: e.tensor_copy(
                                                    onormT[:, hp, c0:c0 + 512], p_bk[:, 0:512]), inc="v")
                                                em.drain("v")
                                            finish_unit(4, merge)
                                        for u in range(SH // 512):
                                            i0 = half * SH + u * 512
                                            jl = list(range(NT))

                                            def sc_ap(h, j, i0=i0):
                                                p0, c = hd(h)
                                                return (lambda p0=p0, c=c, j=j:
                                                        kT[p0:p0 + 64, c, j * 128:(j + 1) * 128],
                                                        lambda p0=p0, c=c, i0=i0:
                                                        qT[p0:p0 + 64, c, i0:i0 + 512], 0)

                                            def av_st(h, j):
                                                return lambda h=h, j=j: v_c[:, j, h * 65:h * 65 + 65]

                                            score_av_loop(hp, jl, 4, sc_ap, av_st, mid_hook=run_pend_fwd)

                                            def merge(hp=hp, i0=i0, half=half):
                                                c0 = i0 - half * SH
                                                em.op("v", lambda e, hp=hp, c0=c0: e.scalar_tensor_tensor(
                                                    onormT[:, hp, c0:c0 + 512], p_bk[:, 0:512], 1.0,
                                                    onormT[:, hp, c0:c0 + 512], ALU.mult, ALU.add),
                                                    inc="v")
                                                em.drain("v")
                                            finish_unit(4, merge)
                                        for g in range(DIL):
                                            ip0 = half * (ND // 2)
                                            jl = list(range(NJD))

                                            def sc_ap(h, j, g=g, ip0=ip0):
                                                p0, c = hd(h)
                                                return (lambda p0=p0, c=c, j=j, g=g: kT[
                                                    p0:p0 + 64, c,
                                                    g + DIL * (j * 128): g + DIL * (j * 128 + 127) + 1: DIL],
                                                    lambda p0=p0, c=c, g=g, ip0=ip0: qT[
                                                        p0:p0 + 64, c,
                                                        g + DIL * ip0: g + DIL * (ip0 + NIS_D * 128 - 1) + 1: DIL],
                                                    0)

                                            def av_st(h, j, g=g):
                                                return (lambda h=h, j=j, g=g:
                                                        v_d[:, g * NJD + j, h * 65:h * 65 + 65])

                                            score_av_loop(hp, jl, NIS_D, sc_ap, av_st, mid_hook=run_pend_fwd)

                                            def merge(hp=hp, g=g, ip0=ip0, half=half):
                                                for k in range(NIS_D):
                                                    c0 = g + DIL * (ip0 + k * 128) - half * SH
                                                    em.op("v", lambda e, hp=hp, c0=c0, k=k:
                                                          e.scalar_tensor_tensor(
                                                              onormT[:, hp, c0:c0 + DIL * 127 + 1:DIL],
                                                              p_bk[:, k * 128:(k + 1) * 128], 1.0,
                                                              onormT[:, hp, c0:c0 + DIL * 127 + 1:DIL],
                                                              ALU.mult, ALU.add), inc="v")
                                                    em.drain("v")
                                            finish_unit(NIS_D, merge)

                                    run_pend_fwd()
                                    run_pend_rest()
                                attn_v = em.val("v")
                                with ExitStack() as p6:
                                    p_s6 = [p6.enter_context(
                                        nc.psum_tensor(f"ps6{s}{half}{i}", [128, DBW], F32))
                                        for i in range(2)]
                                    em.wait("t", "v", attn_v)
                                    em.wait("t", "s", em.val("s"))
                                    em.wait("t", "v", wout_cast)
                                    if half == 1 or s == 1:
                                        em.wait("v", "gsb", em.val("gsb"))
                                    ps_free = [0, 0]
                                    pc = 0
                                    for tt in range(NTH):
                                        for db in range(NDB):
                                            par = pc % 2
                                            pc += 1
                                            em.wait("t", "v", ps_free[par])
                                            for dc in range(DC):
                                                em.op("t", lambda e, tt=tt, db=db, dc=dc, par=par:
                                                      e.matmul(p_s6[par][:, :],
                                                               onormT[:, dc, tt * 128:(tt + 1) * 128],
                                                               woutb[:, dc, db * DBW:(db + 1) * DBW],
                                                               start=(dc == 0), stop=(dc == DC - 1)),
                                                      inc="t" if dc == DC - 1 else None)
                                            em.wait("v", "t", em.val("t"))
                                            em.drain("v")
                                            ps_free[par] = em.op(
                                                "v", lambda e, tt=tt, db=db, par=par: e.tensor_copy(
                                                    s_half[:, tt, db * DBW:(db + 1) * DBW],
                                                    p_s6[par][:, :]), inc="v")
                                    em.wait("g", "v", em.val("v"))
                                    em.op("g", lambda e, s=s, half=half: e.dma_start(
                                        out=s_h[s].rearrange("(t p) d -> p t d", p=128)[
                                            :, half * NTH:(half + 1) * NTH, :],
                                        in_=s_half[:]), inc="gsb", incn=16)

                        _do_half()
                    pa.close()
                    with ExitStack() as p7:
                        sb7 = lambda name, shape, dt: p7.enter_context(
                            nc.sbuf_tensor(name, shape, dt))
                        wupT = sb7(f"wupT{s}", [128, DC, DFF], BF16)
                        xblk = [sb7(f"xblk{s}{i}", [128, TBT, D], F32) for i in range(2)]
                        xnblk = [sb7(f"xnblk{s}{i}", [128, TBT, D], BF16) for i in range(2)]
                        xnormT = sb7(f"xnormT{s}", [128, DC, TB], BF16)
                        h1gT = sb7(f"h1gT{s}", [128, FT, TB], BF16)
                        hsq = [sb7(f"hsq{s}{i}", [128, TB], BF16) for i in range(2)]
                        hacc = sb7(f"hacc{s}", [128, TB], F32)
                        gblk = sb7(f"gblk{s}", [128, TBT, D], BF16)
                        oblk_1 = sb7(f"oblk{s}", [128, TBT, D], F32)
                        oblk = [oblk_1, oblk_1]
                        sblk = [sb7(f"sblk{s}{i}", [128, TBT, D], BF16) for i in range(2)]
                        ssqr = sb7(f"ssqr{s}", [1, TB], F32)
                        sqs = sb7(f"sqs{s}", [128, TBT], F32)
                        alpha2 = sb7(f"alpha2{s}", [128, TBT], F32)
                        a2tmp = sb7(f"a2tmp{s}", [128, TBT], F32)
                        wdnc = [sb7(f"wdnc{s}{i}", [128, DBW], BF16) for i in range(6)]
                        p_h1 = [p7.enter_context(nc.psum_tensor(f"ph1{s}{i}", [128, TB], F32))
                                for i in range(2)]
                        p_ssq = p7.enter_context(nc.psum_tensor(f"pssq{s}", [1, TB], F32))
                        p_ssc = p7.enter_context(nc.psum_tensor(f"pssc{s}", [128, TBT], F32))
                        p_dn = [p7.enter_context(nc.psum_tensor(f"pdn{s}{i}", [128, DBW], F32))
                                for i in range(4)]
                        p_xt = p_dn[0]

                        em.wait("q", "gwa", tern_gw[0])
                        em.wait("q", "gwb", tern_gw[1])
                        em.wait("q", "t", em.val("t"))
                        em.wait("q", "v", em.val("v"))
                        em.wait("q", "s", em.val("s"))
                        for dc in range(DC):
                            em.op("q", lambda e, dc=dc: e.dma_start_transpose(
                                out=wupT[:, dc, :], in_=wupt_h[:, dc * 128:(dc + 1) * 128]),
                                inc=qu, incn=16)
                        wupT_ld = em.val(qu)

                        prev = dict(fin=0, dn=0, sil=0, sq=0, ssc=0, sq2=0, ssqmm=0)
                        wdn_guard = [0] * 6
                        sblk_guard = [0, 0]
                        xblk_guard = [0, 0]
                        xnblk_guard = [0, 0]
                        xn_info = {}
                        pend_fin = [None]
                        em.wait("v", "gsb", em.val("gsb"))
                        em.wait("s", "gsb", em.val("gsb"))
                        p7v0, p7t0, p7s0 = em.val("v"), em.val("t"), em.val("s")
                        prev["gelu"] = p7s0
                        GP = DBW // 128
                        NGRP = TBT // GP

                        def emit_loads_xn(blk):
                            t0 = blk * TBT
                            spar = blk % 2
                            em.wait("g", "v", xblk_guard[spar] if blk >= 2 else p7v0)
                            if blk == 0:
                                em.wait("g", "t", p7t0)
                                em.wait("g", "s", p7s0)
                            em.op("g", lambda e, s=s, t0=t0, spar=spar: e.dma_start(
                                out=xblk[spar][:],
                                in_=x_h[s].rearrange("(t p) d -> p t d", p=128)[:, t0:t0 + TBT, :]),
                                inc=["gxa", "gxb"][spar], incn=16)
                            x_ld2 = em.val(["gxa", "gxb"][spar])
                            em.wait("g", "gsb", em.val("gsb"))
                            em.wait("g", "v", sblk_guard[spar])
                            em.op("g", lambda e, s=s, t0=t0, spar=spar: e.dma_start(
                                out=sblk[spar][:],
                                in_=s_h[s].rearrange("(t p) d -> p t d", p=128)[:, t0:t0 + TBT, :]),
                                inc=["qs0", "qs1"][spar], incn=16)
                            s_ld = em.val(["qs0", "qs1"][spar])
                            em.wait("v", ["gxa", "gxb"][spar], x_ld2)
                            em.wait("v", "t", xnblk_guard[spar])
                            for tt in range(TBT):
                                em.drain("v")
                                em.op("v", lambda e, tt=tt, t0=t0, spar=spar: e.tensor_scalar(
                                    xnblk[spar][:, tt, :], xblk[spar][:, tt, :],
                                    stat_nm[:, t0 + tt:t0 + tt + 1],
                                    stat_is[:, t0 + tt:t0 + tt + 1], ALU.add, ALU.mult), inc="v")
                                em.drain("v")
                                em.op("v", lambda e, tt=tt, t0=t0, spar=spar: e.tensor_scalar(
                                    xnblk[spar][:, tt, :], xnblk[spar][:, tt, :],
                                    stat_a1[:, t0 + tt:t0 + tt + 1], None, ALU.mult), inc="v")
                            xn_info[blk] = (em.val("v"), s_ld)

                        emit_loads_xn(0)
                        for blk in range(NB):
                            t0 = blk * TBT
                            spar = blk % 2
                            xnorm_v, s_ld = xn_info.pop(blk)
                            em.wait("t", "v", xnorm_v)
                            em.wait("t", "s", prev["gelu"])
                            pxt_guard = {}
                            for dc in range(DC):
                                for grp in range(NGRP):
                                    bank = (dc * NGRP + grp) % 2
                                    em.wait("t", "v", pxt_guard.get(bank, 0))
                                    for k in range(GP):
                                        tt = grp * GP + k
                                        em.op("t", lambda e, dc=dc, tt=tt, k=k, bank=bank,
                                              spar=spar: e.matmul(
                                                  p_h1[bank][:, k * 128:(k + 1) * 128],
                                                  xnblk[spar][:, tt, dc * 128:(dc + 1) * 128],
                                                  ident[:, 0:128], start=True, stop=True),
                                              inc="t" if k == GP - 1 else None)
                                    em.wait("v", "t", em.val("t"))
                                    em.drain("v")
                                    pxt_guard[bank] = em.op(
                                        "v", lambda e, dc=dc, grp=grp, bank=bank: e.tensor_copy(
                                            xnormT[:, dc, grp * DBW:(grp + 1) * DBW],
                                            p_h1[bank][:, 0:DBW]), inc="v")
                            xnT_v = em.val("v")
                            xnblk_guard[spar] = em.val("t")
                            if pend_fin[0] is not None:
                                pend_fin[0]()
                                pend_fin[0] = None
                            em.wait("t", "v", xnT_v)
                            if blk == 0:
                                em.wait("t", qu, wupT_ld)
                            gelu = [0] * FT
                            for ft in range(FT):
                                par = ft % 2
                                if ft >= 2:
                                    em.wait("t", "s", gelu[ft - 2])
                                for dc in range(DC):
                                    em.op("t", lambda e, ft=ft, dc=dc, par=par: e.matmul(
                                        p_h1[par][:, :], wupT[:, dc, ft * 128:(ft + 1) * 128],
                                        xnormT[:, dc, :], start=(dc == 0), stop=(dc == DC - 1)),
                                        inc="t" if dc == DC - 1 else None)
                                em.wait("s", "t", em.val("t"))
                                if ft == 0:
                                    em.wait("s", "t", prev["dn"])
                                em.drain("s")
                                gelu[ft] = em.op("s", lambda e, ft=ft, par=par: e.activation(
                                    h1gT[:, ft, :], p_h1[par][:, :], AF.Gelu), inc="s")
                                em.wait("v", "s", gelu[ft])
                                if ft == 0:
                                    em.wait("v", "t", prev["ssqmm"])
                                em.drain("v")
                                em.op("v", lambda e, ft=ft, par=par: e.tensor_tensor(
                                    hsq[par][:], h1gT[:, ft, :], h1gT[:, ft, :], ALU.mult),
                                    inc="v")
                                em.drain("v")
                                if ft == 0:
                                    em.op("v", lambda e: e.tensor_copy(hacc[:], hsq[0][:]),
                                          inc="v")
                                else:
                                    em.op("v", lambda e, par=par: e.tensor_tensor(
                                        hacc[:], hacc[:], hsq[par][:], ALU.add), inc="v")
                                em.drain("v")
                            gelu_all = gelu[FT - 1]
                            prev["gelu"] = gelu_all
                            em.op("v", lambda e: e.tensor_copy(hsq[0][:], hacc[:]), inc="v")
                            hacc_v = em.val("v")
                            em.wait("t", "v", hacc_v)
                            em.wait("t", "v", prev["sq"])
                            em.op("t", lambda e: e.matmul(
                                p_ssq[0:1, :], ones_cb[:, 0:1], hsq[0][:],
                                start=True, stop=True), inc="t")
                            prev["ssqmm"] = ssq_t = em.val("t")
                            sil_s = prev["sil"]

                            def emit_down_mms(db, sil_guard):
                                for fc in range(FT):
                                    wpar = fc % 6
                                    if s + blk + db + fc == 0:
                                        em.wait("q", "gwa", tern_gw[0])
                                        em.wait("q", "gwb", tern_gw[1])
                                    em.wait("q", "t", wdn_guard[wpar])
                                    em.op("q", lambda e, fc=fc, db=db, wpar=wpar:
                                          e.dma_start_transpose(
                                              out=wdnc[wpar][:],
                                              in_=wdnt_h[db * DBW:(db + 1) * DBW,
                                                         fc * 128:(fc + 1) * 128]),
                                          inc=f"qd{wpar}", incn=16)
                                    em.wait("t", f"qd{wpar}", em.val(f"qd{wpar}"))
                                    if fc == 0:
                                        em.wait("t", "s", gelu_all)
                                        em.wait("t", "s", sil_guard)
                                    for tt in range(TBT):
                                        em.op("t", lambda e, fc=fc, tt=tt, wpar=wpar: e.matmul(
                                            p_dn[tt][:, :], h1gT[:, fc, tt * 128:(tt + 1) * 128],
                                            wdnc[wpar][:, :], start=(fc == 0), stop=(fc == FT - 1)),
                                            inc="t" if tt == TBT - 1 else None)
                                    wdn_guard[wpar] = em.val("t")
                                return em.val("t")

                            dn_t0 = emit_down_mms(0, sil_s)
                            em.wait("v", "t", ssq_t)
                            em.wait("v", "t", prev["ssc"])
                            em.drain("v")
                            prev["sq"] = em.op(
                                "v", lambda e: e.tensor_copy(ssqr[0:1, :], p_ssq[0:1, :]), inc="v")
                            em.wait("t", "v", prev["sq"])
                            em.wait("t", "s", prev["sq2"])
                            for tt in range(TBT):
                                em.op("t", lambda e, tt=tt: e.matmul(
                                    p_ssc[:, tt:tt + 1], ssqr[0:1, tt * 128:(tt + 1) * 128],
                                    one11[0:1, 0:1], start=True, stop=True),
                                    inc="t" if tt == TBT - 1 else None)
                            prev["ssc"] = em.val("t")
                            em.wait("s", "t", prev["ssc"])
                            em.drain("s")
                            prev["sq2"] = em.op(
                                "s", lambda e: e.activation(sqs[:, :], p_ssc[:, :], AF.Sqrt), inc="s")
                            em.wait("v", "s", prev["sq2"])
                            em.wait("v", "s", prev["sil"])
                            em.drain("v")
                            em.op("v", lambda e: e.tensor_scalar(
                                a2tmp[:, :], sqs[:, :], 1.0 / float(np.sqrt(DFF)), 1e-8,
                                ALU.mult, ALU.add), inc="v")
                            em.drain("v")
                            em.op("v", lambda e: e.reciprocal(alpha2[:, :], a2tmp[:, :]), inc="v")
                            a2_v = em.val("v")
                            if blk + 1 < NB:
                                emit_loads_xn(blk + 1)
                            em.wait("s", "t", dn_t0)
                            em.wait("s", "v", a2_v)
                            em.wait("s", "v", prev["fin"])
                            for tt in range(TBT):
                                em.drain("s")
                                em.op("s", lambda e, tt=tt: e.activation(
                                    gblk[:, tt, 0:DBW], p_dn[tt][:, :],
                                    AF.Silu, scale=alpha2[:, tt:tt + 1]), inc="s")
                            sil_s = em.val("s")
                            dn_t1 = emit_down_mms(1, sil_s)
                            em.wait("s", "t", dn_t1)
                            for tt in range(TBT):
                                em.drain("s")
                                em.op("s", lambda e, tt=tt: e.activation(
                                    gblk[:, tt, DBW:2 * DBW], p_dn[tt][:, :],
                                    AF.Silu, scale=alpha2[:, tt:tt + 1]), inc="s")
                            sil_s = em.val("s")
                            prev["dn"] = em.val("t")
                            prev["sil"] = sil_s

                            def fin_fn(blk=blk, spar=spar, s_ld=s_ld, sil_s=sil_s, t0=t0):
                                em.wait("v", "s", sil_s)
                                em.wait("v", ["qs0", "qs1"][spar], s_ld)
                                em.wait("v", "goa", em.val("goa"))
                                em.wait("v", "gob", em.val("gob"))
                                for tt in range(TBT):
                                    em.drain("v")
                                    em.op("v", lambda e, tt=tt, spar=spar: e.tensor_tensor(
                                        oblk[spar][:, tt, :], sblk[spar][:, tt, :], gblk[:, tt, :],
                                        ALU.mult), inc="v")
                                    em.drain("v")
                                    em.op("v", lambda e, tt=tt, spar=spar: e.tensor_tensor(
                                        oblk[spar][:, tt, :], oblk[spar][:, tt, :],
                                        xblk[spar][:, tt, :], ALU.add), inc="v")
                                fin_v = em.val("v")
                                sblk_guard[spar] = fin_v
                                xblk_guard[spar] = fin_v
                                prev["fin"] = fin_v
                                em.wait("g", "v", fin_v)
                                em.op("g", lambda e, s=s, t0=t0, spar=spar: e.dma_start(
                                    out=out_h[s].rearrange("(t p) d -> p t d", p=128)[
                                        :, t0:t0 + TBT, :],
                                    in_=oblk[spar][:]), inc=["goa", "gob"][spar], incn=16)
                            pend_fin[0] = fin_fn
                        if pend_fin[0] is not None:
                            pend_fin[0]()
                            pend_fin[0] = None

            _do_seq()
        em.wait("g", "goa", em.val("goa"))
        em.wait("g", "gob", em.val("gob"))

        with nc.Block() as block:
            @block.tensor
            def _(eng):
                for f in em.ops["t"]:
                    f(eng)

            @block.scalar
            def _(eng):
                for f in em.ops["s"]:
                    f(eng)

            @block.vector
            def _(eng):
                for f in em.ops["v"]:
                    f(eng)

            @block.sync
            def _(eng):
                for f in em.ops["q"]:
                    f(eng)

            @block.gpsimd
            def _(eng):
                for f in em.ops["g"]:
                    f(eng)

    return nc


def kernel(L0_attn_x, L0_attn_w_qkv, L0_attn_w_out, L0_ffn_w_up, L0_ffn_w_down):
    x = np.ascontiguousarray(L0_attn_x, dtype=np.float32)
    wqkv = np.ascontiguousarray(L0_attn_w_qkv, dtype=np.float32)
    wout = np.ascontiguousarray(L0_attn_w_out, dtype=np.float32)
    wup = np.ascontiguousarray(L0_ffn_w_up, dtype=np.float32)
    wdn = np.ascontiguousarray(L0_ffn_w_down, dtype=np.float32)
    ident = np.eye(128, dtype=ml_dtypes.bfloat16)
    n = 8
    nc = build(FULL)
    in_maps = [{
        "x": x[2 * c:2 * c + 2],
        "w_qkv": wqkv, "w_out": wout, "w_up": wup, "w_down": wdn,
        "ident": ident,
    } for c in range(n)]
    res = run_bass_kernel_spmd(nc, in_maps, core_ids=list(range(n)))
    return np.concatenate([r["out"] for r in res.results], axis=0)
